# version 08 (final)
# baseline (speedup 1.0000x reference)
import os
import numpy as np
import concourse.bass as bass
import concourse.mybir as mybir
import bass_rust
from concourse.bass_utils import run_bass_kernel_spmd

F32 = mybir.dt.float32
BF16 = mybir.dt.bfloat16
I16 = mybir.dt.int16
ALU = mybir.AluOpType
ACT = mybir.ActivationFunctionType
AX = mybir.AxisListType
ROMAX = bass_rust.ReduceOp.max

NCORE = 8
MK_DRAIN = os.environ.get("MK_DRAIN", "1") == "1"
D = 1024
F = 4096
TLOC = 2048
B = 2
S = 8192
CH = 64
NCH = 128
NGRP = 32
EPS_LN = 1e-5
EPS_Q = 1e-8

ENGS = ("sync", "vector", "scalar", "tensor", "gpsimd")

N_SUB = int(os.environ.get("MK_SUBLAYERS", "4"))
AA_STAGE = int(os.environ.get("MK_AA_STAGE", "9"))
SIGF = ACT.Sigmoid if os.environ.get("MK_SIG", "1") == "1" else ACT.Identity
AA2 = int(os.environ.get("MK_AA2", "3"))
NOMM = os.environ.get("MK_NOMM", "0") == "1"


def _ap(handle, offset, dims):
    return bass.AP(tensor=handle, offset=offset, ap=[[s, c] for (s, c) in dims])


class Bld:
    def __init__(self, nc):
        self.nc = nc
        self.q = {e: [] for e in ENGS}
        self.semh = {}
        self.cnt = {}
        self.nbar = 0
        self.pid = {}
        self.clean = {e: False for e in ENGS}

    def sem(self, name):
        if name not in self.cnt:
            self.cnt[name] = 0
        return name

    def add(self, eng, fn):
        self.q[eng].append(fn)

    def op(self, eng, fn, sem=None, amt=1, waits=(), drain=False):
        for w in waits:
            if w is None:
                continue
            s, v = w
            if s.startswith("d_"):
                v = self.cnt.get(s, 0)
            if v <= 0:
                continue
            self.sem(s)
            self.add(eng, lambda e, s=s, v=v: e.wait_ge(self.semh[s], v))
        if drain and MK_DRAIN and not self.clean[eng]:
            self.add(eng, lambda e: e.drain())
            self.clean[eng] = True
        if sem is not None:
            self.sem(sem)
            self.cnt[sem] += amt
            tok = (sem, self.cnt[sem])
            self.add(eng, lambda e, fn=fn, s=sem, a=amt: fn(e).then_inc(self.semh[s], a))
            self.clean[eng] = False
            return tok
        self.add(eng, fn)
        self.clean[eng] = False
        return None

    def dma(self, eng, out, in_, waits=(), chan=None, **kw):
        sem = ("ch_" + chan) if chan else ("d_" + eng)
        return self.op(eng, lambda e: e.dma_start(out=out, in_=in_, **kw),
                       sem=sem, amt=16, waits=waits)

    def dmat(self, eng, out, in_, waits=()):
        return self.op(eng, lambda e: e.dma_start_transpose(out, in_),
                       sem="d_" + eng, amt=16, waits=waits)

    def mark(self, eng, sem):
        tok = self.op(eng, lambda e: e.drain(), sem=sem)
        self.clean[eng] = True
        return tok

    def pemark(self, sem):
        return self.op("tensor", lambda e: e.nop(), sem=sem)

    def barrier(self):
        self.nbar += 1
        self.sem("bar")
        for eng in ENGS:
            dn = "d_" + eng
            if self.cnt.get(dn, 0) > 0:
                self.add(eng, lambda e, s=dn, v=self.cnt[dn]: e.wait_ge(self.semh[s], v))
            if eng == "sync":
                for cn in list(self.cnt.keys()):
                    if cn.startswith("ch_") and self.cnt[cn] > 0:
                        self.add(eng, lambda e, s=cn, v=self.cnt[cn]: e.wait_ge(self.semh[s], v))
            if eng == "gpsimd" and self.cnt.get("cc", 0) > 0:
                self.add(eng, lambda e, v=self.cnt["cc"]: e.wait_ge(self.semh["cc"], v))
            self.add(eng, lambda e: e.drain())
            self.add(eng, lambda e: e.sem_inc(self.semh["bar"], 1))
        self.cnt["bar"] += len(ENGS)
        tgt = self.cnt["bar"]
        for eng in ENGS:
            self.add(eng, lambda e, t=tgt: e.wait_ge(self.semh["bar"], t))
            self.clean[eng] = True

    @staticmethod
    def xtg_par(tensor_handle, gi, dt, jj):
        ap = tensor_handle.ap()
        pstep = ap.ap[0][0]
        return bass.AP(tensor=ap.tensor, offset=dt * 1024 + gi * 256 + 64 * jj,
                       ap=[[pstep, 128], [8192, 2], [1, 64]])

    @staticmethod
    def xtg_sl(tensor_handle, base, dt, jj):
        ap = tensor_handle.ap()
        pstep = ap.ap[0][0]
        return bass.AP(tensor=ap.tensor, offset=base + dt * 512 + 64 * jj,
                       ap=[[pstep, 128], [256, 2], [1, 64]])

    @staticmethod
    def sb4(tensor_handle, col0, dims):
        ap = tensor_handle.ap()
        pstep = ap.ap[0][0]
        return bass.AP(tensor=ap.tensor, offset=col0,
                       ap=[[pstep, 128]] + [[st, ct] for (st, ct) in dims])

    def getpid(self, eng, key=None):
        k = (eng, key)
        if k not in self.pid:
            holder = {}

            def mk(e, h=holder):
                h["sv"] = e.snap(self.nc.partition_id(), min_val=0, max_val=7)
            self.add(eng, mk)
            self.pid[k] = holder
        return self.pid[k]


def build_nc():
    nc = bass.Bass("TRN2", target_bir_lowering=False,
                   detect_race_conditions=os.environ.get("MK_RACE", "0") == "1")
    dt_ = nc.dram_tensor

    x_in = dt_("x_in", [TLOC, D], F32, kind="ExternalInput")
    wz_in = dt_("wz_in", [D, D], F32, kind="ExternalInput")
    wh_in = dt_("wh_in", [D, D], F32, kind="ExternalInput")
    bz_in = dt_("bz_in", [128, 8], F32, kind="ExternalInput")
    bh_in = dt_("bh_in", [128, 8], F32, kind="ExternalInput")
    w0s_in = dt_("w0s_in", [128, D], F32, kind="ExternalInput")
    wvs_in = dt_("wvs_in", [128, D], F32, kind="ExternalInput")
    bkv_in = dt_("bkv_in", [1, 128], F32, kind="ExternalInput")
    eta_in = dt_("eta_in", [1, 1], F32, kind="ExternalInput")
    w1_in = dt_("w1_in", [F, D], F32, kind="ExternalInput")
    w2_in = dt_("w2_in", [D, F], F32, kind="ExternalInput")
    b1_in = dt_("b1_in", [128, 32], F32, kind="ExternalInput")
    b2_in = dt_("b2_in", [128, 8], F32, kind="ExternalInput")
    pmask_in = dt_("pmask_in", [128, 4], F32, kind="ExternalInput")
    gmask_in = dt_("gmask_in", [128, 2], F32, kind="ExternalInput")

    out_d = dt_("out_d", [TLOC, D], F32, kind="ExternalOutput")

    wzT_d = dt_("wzT_d", [D, D], BF16)
    whT_d = dt_("whT_d", [D, D], BF16)
    qw1T_d = dt_("qw1T_d", [D, F], BF16)
    qw2T_d = dt_("qw2T_d", [F, D], BF16)
    h_d = dt_("h_d", [F, TLOC], BF16)
    o2_d = dt_("o2_d", [D, TLOC], BF16)
    ca_in = dt_("ca_in", [128, 16], F32)
    ca_out = dt_("ca_out", [128 * NCORE, 16], F32)
    qm_in = [dt_(f"qm_in{i}", [1, 1], F32) for i in range(4)]
    qm_out = [dt_(f"qm_out{i}", [NCORE, 1], F32) for i in range(4)]
    xtok_in = dt_("xtok_in", [TLOC, D], BF16)
    xtok_all = dt_("xtok_all", [TLOC * NCORE, D], BF16)
    xt_in = dt_("xt_in", [D, TLOC], BF16)
    xt_all = dt_("xt_all", [D * NCORE, TLOC], BF16)
    tt_in = dt_("tt_in", [16 * 128, 128], BF16)
    tt_out = dt_("tt_out", [NCH * 128, 128], BF16)
    ystage = dt_("ystage", [B * S, 128], F32)
    bc_d = dt_("bc_d", [1, 8], F32)
    dbg_d = dt_("dbg_d", [128, 1024], BF16)
    pm_d = dt_("pm_d", [128, 1], F32)
    yall = dt_("yall", [B * S * NCORE, 128], F32)

    bld = Bld(nc)
    from contextlib import ExitStack
    with ExitStack() as stack:
        sb = lambda name, shape, dt: stack.enter_context(nc.sbuf_tensor(name, shape, dt))
        XRES = sb("XRES", [128, 16 * 1024], F32)
        AR1 = sb("AR1", [128, 32768], BF16)
        AR2 = sb("AR2", [128, 16384], BF16)
        SCR = sb("SCR", [128, 2048], F32)
        W0T = sb("W0T", [128, D], BF16)
        WVT = sb("WVT", [128, D], BF16)
        WD0 = sb("WD0", [128, D], BF16)
        WDS = sb("WDS", [128, D], BF16)
        IDN = sb("IDN", [128, 128], BF16)
        ONESB = sb("ONESB", [128, 128], BF16)
        ONESROW = sb("ONESROW", [1, 128], BF16)
        BKVR = sb("BKVR", [1, 128], BF16)
        NBKVR = sb("NBKVR", [1, 128], BF16)
        NEGC = sb("NEGC", [128, 1], F32)
        ETA1 = sb("ETA1", [1, 1], F32)
        SW1 = sb("SW1", [128, 32], F32)
        SW2 = sb("SW2", [128, 8], F32)
        SC1 = sb("SC1", [128, 32], F32)
        SC2 = sb("SC2", [128, 8], F32)
        RBQ = sb("RBQ", [128, 32], F32)
        QBI = sb("QBI", [128, 32], F32)
        B1Q = sb("B1Q", [128, 32], F32)
        B2Q = sb("B2Q", [128, 8], F32)
        B1V = sb("B1V", [128, 32], F32)
        B2V = sb("B2V", [128, 8], F32)
        BZS = sb("BZS", [128, 8], F32)
        BHS = sb("BHS", [128, 8], F32)
        PMSK = sb("PMSK", [128, 4], F32)
        STAT = sb("STAT", [128, 16], F32)
        STSQ = sb("STSQ", [128, 16], F32)
        STMX = sb("STMX", [128, 16], F32)
        MEAN = sb("MEAN", [128, 16], F32)
        VARB = sb("VARB", [128, 16], F32)
        RSD = sb("RSD", [128, 16], F32)
        NMR = sb("NMR", [128, 16], F32)
        LMX = sb("LMX", [128, 16], F32)
        SCQ = sb("SCQ", [128, 16], F32)
        BSQ = sb("BSQ", [128, 16], F32)
        GM = sb("GM", [128, 1], F32)
        GMA = sb("GMA", [128, 1], F32)
        GMW = sb("GMW", [128, 1], F32)
        GMB = sb("GMB", [128, 1], F32)
        GVP = sb("GVP", [1, 128], F32)
        GS = sb("GS", [1, 1], F32)
        RQ = sb("RQ", [1, 1], F32)
        RQB = sb("RQB", [128, 1], F32)
        SQB = sb("SQB", [128, 1], F32)
        QTI = sb("QTI", [128, 2 * 1024], I16)
        QTB = sb("QTB", [128, 2 * 1024], BF16)
        HSB = sb("HSB", [128, 2 * 512], BF16)
        MAXA = sb("MAXA", [128, 512], BF16)
        CAB = sb("CAB", [128, 16], F32)
        CAG = sb("CAG", [128, 64], F32)
        GMSK = sb("GMSK", [128, 2], F32)
        CARR = sb("CARR", [128, 8], F32)
        CCB = sb("CCB", [128, 4], F32)
        CCT = sb("CCT", [128, 1], F32)
        WZC = sb("WZC", [128, 1024], BF16)
        WHC = sb("WHC", [128, 1024], BF16)
        TTG = sb("TTG", [128, 2 * 512], BF16)
        VSB = sb("VSB", [128, 2 * 128], BF16)
        USB = sb("USB", [128, 2 * 128], BF16)
        ENCS = sb("ENCS", [128, 2 * 128], BF16)
        YSB = sb("YSB", [128, 2 * 128], F32)
        CHN = sb("CHN", [128, 4 * 8 * 128], BF16)
        PA = stack.enter_context(nc.psum_tensor("PA", [128, 512], F32))
        PB = stack.enter_context(nc.psum_tensor("PB", [128, 512], F32))
        PU0 = stack.enter_context(nc.psum_tensor("PU0", [128, 512], F32))
        PU1 = stack.enter_context(nc.psum_tensor("PU1", [128, 512], F32))
        PEa = stack.enter_context(nc.psum_tensor("PEa", [128, 512], F32))
        PEb = stack.enter_context(nc.psum_tensor("PEb", [128, 512], F32))
        PWD = stack.enter_context(nc.psum_tensor("PWD", [128, 1024], F32))

        for s in ["d_sync", "d_scalar", "d_gpsimd", "cc", "bar",
                  "mm1", "ep1", "vev", "mm2", "ep2", "q2b",
                  "vmm", "umm", "uev", "emm", "eev", "yev",
                  "wmm", "wrs", "wrv", "prl", "lnq", "sqv", "gq", "rad",
                  "ch_gl0", "ch_gl1", "ch_h0", "ch_h1", "ch_m0", "ch_m1", "tpm", "tpv"]:
            bld.sem(s)
        for name in list(bld.cnt.keys()):
            bld.semh[name] = stack.enter_context(nc.semaphore(name))

        def xres(tt):
            return XRES[:, 1024 * tt:1024 * (tt + 1)]

        def tp_tiles(items, banks=(PEa, PEb)):
            for i, (src, dst, w) in enumerate(items):
                bank = banks[i % 2]
                wts = list(w)
                if getattr(bld, "idnf_tok", None):
                    wts.append(bld.idnf_tok)
                wts.append(("tpv", bld.cnt["tpv"] - 1))
                tk = bld.op("tensor", lambda e, src=src, bank=bank: e.matmul(
                    bank[:, 0:128], src, IDN[:, :], start=True, stop=True),
                    sem="tpm", waits=wts)
                bld.op("vector", lambda e, dst=dst, bank=bank: e.tensor_copy(
                    dst, bank[:, 0:128]), waits=[tk], drain=True)
                bld.mark("vector", "tpv")
            return ("tpv", bld.cnt["tpv"])

        def ar1(c0, n):
            return AR1[:, c0:c0 + n]

        def ar2(c0, n):
            return AR2[:, c0:c0 + n]

        bld.dma("sync", XRES[:, :], _ap(x_in, 0, [(1024, 128), (131072, 16), (1, 1024)]))
        bld.dma("sync", BZS[:, :], bz_in[:, :])
        bld.dma("sync", BHS[:, :], bh_in[:, :])
        bld.dma("sync", PMSK[:, :], pmask_in[:, :])
        bld.dma("sync", GMSK[:, :], gmask_in[:, :])
        bld.dma("sync", B1V[:, :], b1_in[:, :])
        bld.dma("sync", B2V[:, :], b2_in[:, :])
        t_bkv = bld.dma("gpsimd", BKVR[0:1, :], bkv_in[:, :])
        t_eta = bld.dma("sync", ETA1[:, :], eta_in[:, :])

        bld.op("vector", lambda e: e.tensor_scalar(GS[0:1, 0:1], ETA1[0:1, 0:1],
               -1.0 / 1024.0, None, ALU.mult), waits=[t_eta], drain=True)
        bld.op("vector", lambda e: e.memset(ONESROW[:, :], 1.0), drain=True)
        bld.op("vector", lambda e: e.memset(ONESB[:, :], 1.0), drain=True)
        bld.op("vector", lambda e: e.tensor_scalar(NBKVR[0:1, :], BKVR[0:1, :],
               -1.0, None, ALU.mult), waits=[t_bkv], drain=True)
        tv0 = bld.mark("vector", "vev")
        tnc = bld.dma("sync", bc_d[0:1, 2:3], GS[0:1, 0:1], waits=[tv0])
        tnc2 = bld.dma("sync", NEGC[:, 0:1], _ap(bc_d, 2, [(0, 128), (1, 1)]), waits=[tnc])
        tvb = tnc2
        tvm = bld.mark("vector", "vev")
        bld.op("gpsimd", lambda e: e.affine_select(
            IDN[:, :], ONESB[:, :], [[-1, 128]], ALU.is_equal, 0.0,
            base=0, channel_multiplier=1), waits=[tvb, tvm], drain=True)
        bld.idnf_tok = bld.mark("gpsimd", "prl")

        for (w_in, wT_d) in [(wz_in, wzT_d), (wh_in, whT_d)]:
            for rt in range(8):
                sl = SCR[:, 0:1024] if rt % 2 == 0 else SCR[:, 1024:2048]
                w = [("sqv", bld.cnt["sqv"] - 1)] if (rt >= 2 or w_in is wh_in) else ()
                tl = bld.dma("sync", sl, w_in[128 * rt:128 * (rt + 1), :], waits=w)
                bld.op("vector", lambda e, sl=sl: e.tensor_copy(QTB[:, 0:1024], sl),
                       waits=[tl, ("tpv", bld.cnt["tpv"])], drain=True)
                tv = bld.mark("vector", "sqv")
                tpv = tp_tiles([(QTB[:, 128 * ct:128 * (ct + 1)],
                                 QTB[:, 1024 + 128 * ct:1024 + 128 * (ct + 1)],
                                 [tv, ("d_sync", 1)])
                                for ct in range(8)])
                bld.dma("sync", _ap(wT_d, 128 * rt, [(1024, 128), (131072, 8), (1, 128)]),
                        QTB[:, 1024:2048], waits=[tpv])
                bld.mark("vector", "sqv")
        for wi, (ws_in, WT) in enumerate([(w0s_in, W0T), (wvs_in, WVT)]):
            sl = SCR[:, 1024 * wi:1024 * (wi + 1)]
            qd = QTB[:, 1024 * wi:1024 * (wi + 1)]
            tl = bld.dma("sync", sl, ws_in[:, :],
                         waits=[("sqv", bld.cnt["sqv"])])
            bld.op("vector", lambda e, sl=sl, qd=qd: e.tensor_copy(qd, sl),
                   waits=[tl, ("tpv", bld.cnt["tpv"])], drain=True)
            tv = bld.mark("vector", "sqv")
            tp_tiles([(qd[:, 128 * dtile:128 * (dtile + 1)],
                       WT[:, 128 * dtile:128 * (dtile + 1)], [tv])
                      for dtile in range(8)])
            bld.mark("vector", "sqv")
        bld.op("vector", lambda e: e.tensor_sub(WD0[:, :], W0T[:, :], WVT[:, :]), drain=True)

        for ft in range(32):
            sl = SCR[:, 0:1024] if ft % 2 == 0 else SCR[:, 1024:2048]
            w = [("sqv", bld.cnt["sqv"] - 1)] if ft >= 2 else ()
            tl = bld.dma("sync", sl, w1_in[128 * ft:128 * (ft + 1), :], waits=w)
            bld.op("vector", lambda e, sl=sl: e.tensor_reduce(
                GM[:, 0:1], sl, AX.X, ALU.max, apply_absolute_value=True),
                waits=[tl], drain=True)
            bld.op("vector", lambda e, ft=ft: e.tensor_scalar(
                SW1[:, ft:ft + 1], GM[:, 0:1], EPS_Q, 1.0 / 127.0, ALU.max, ALU.mult),
                drain=True)
            bld.op("vector", lambda e: e.reciprocal(GMA[:, 0:1], GM[:, 0:1]), drain=True)
            bld.op("vector", lambda e: e.tensor_scalar(
                GMA[:, 0:1], GMA[:, 0:1], 127.0, None, ALU.mult), drain=True)
            qsl = QTI[:, 0:1024] if ft % 2 == 0 else QTI[:, 1024:2048]
            bsl = QTB[:, 0:1024] if ft % 2 == 0 else QTB[:, 1024:2048]
            bld.op("vector", lambda e, sl=sl, qsl=qsl: e.tensor_scalar(
                qsl, sl, GMA[:, 0:1], None, ALU.mult), drain=True)
            bld.op("vector", lambda e, qsl=qsl, bsl=bsl: e.tensor_copy(bsl, qsl),
                   waits=[("tpv", bld.cnt["tpv"] - 8)], drain=True)
            tq = bld.mark("vector", "sqv")
            tpv = tp_tiles([(bsl[:, 128 * kt:128 * (kt + 1)],
                             CHN[:, 128 * kt:128 * (kt + 1)], [tq, ("d_sync", 1)])
                            for kt in range(8)])
            bld.dma("sync", _ap(qw1T_d, 128 * ft, [(F, 128), (128 * F, 8), (1, 128)]),
                    CHN[:, 0:1024], waits=[tpv])
            bld.mark("vector", "sqv")
        for rt in range(8):
            for hh in range(2):
                w = [("gq", bld.cnt["gq"])] if (rt > 0 or hh > 0) else ()
                tl = bld.dma("sync", SCR[:, 0:2048],
                             w2_in[128 * rt:128 * (rt + 1), 2048 * hh:2048 * (hh + 1)], waits=w)
                if hh == 0:
                    bld.op("vector", lambda e: e.tensor_reduce(
                        GMW[:, 0:1], SCR[:, 0:2048], AX.X, ALU.max, apply_absolute_value=True),
                        waits=[tl], drain=True)
                else:
                    bld.op("vector", lambda e: e.tensor_reduce(
                        GMB[:, 0:1], SCR[:, 0:2048], AX.X, ALU.max, apply_absolute_value=True),
                        waits=[tl], drain=True)
                bld.mark("vector", "gq")
            bld.op("vector", lambda e: e.tensor_tensor(GMW[:, 0:1], GMW[:, 0:1], GMB[:, 0:1], ALU.max), drain=True)
            bld.op("vector", lambda e, rt=rt: e.tensor_scalar(
                SW2[:, rt:rt + 1], GMW[:, 0:1], EPS_Q, 1.0 / 127.0, ALU.max, ALU.mult),
                drain=True)
            bld.op("vector", lambda e: e.reciprocal(GMB[:, 0:1], GMW[:, 0:1]), drain=True)
            bld.op("vector", lambda e: e.tensor_scalar(
                GMB[:, 0:1], GMB[:, 0:1], 127.0, None, ALU.mult), drain=True)
            tvr = bld.mark("vector", "gq")
            for hh in range(2):
                tl = bld.dma("sync", SCR[:, 0:2048],
                             w2_in[128 * rt:128 * (rt + 1), 2048 * hh:2048 * (hh + 1)],
                             waits=[("gq", bld.cnt["gq"])])
                bld.op("gpsimd", lambda e: e.tensor_scalar(
                    QTI[:, 0:2048], SCR[:, 0:2048], GMB[:, 0:1], None, ALU.mult),
                    waits=[tl, tvr], drain=True)
                bld.op("gpsimd", lambda e: e.tensor_copy(QTB[:, 0:2048], QTI[:, 0:2048]),
                       drain=True)
                tq = bld.mark("gpsimd", "gq")
                tpv = tp_tiles([(QTB[:, 128 * k:128 * (k + 1)],
                                 CHN[:, 128 * k:128 * (k + 1)], [tq, ("d_sync", 1)])
                                for k in range(16)])
                bld.dma("sync", _ap(qw2T_d, (2048 * hh * D) + 128 * rt,
                                    [(D, 128), (128 * D, 16), (1, 128)]),
                        CHN[:, 0:2048], waits=[tpv])
                bld.mark("gpsimd", "gq")
        bld.barrier()

        def ln_stats(need_max):
            for tt in range(16):
                sl = SCR[:, 0:1024] if tt % 2 == 0 else SCR[:, 1024:2048]
                w = [("lnq", bld.cnt["lnq"] - 1)] if tt >= 2 else ()
                bld.op("scalar", lambda e, tt=tt, sl=sl: e.activation(
                    sl, xres(tt), ACT.Square), waits=w, drain=True)
                tsq = bld.mark("scalar", "gq")
                bld.op("vector", lambda e, tt=tt: e.reduce_sum(
                    STAT[:, tt:tt + 1], xres(tt), AX.X), drain=True)
                bld.op("vector", lambda e, tt=tt, sl=sl: e.reduce_sum(
                    STSQ[:, tt:tt + 1], sl, AX.X), waits=[tsq], drain=True)
                if need_max:
                    bld.op("vector", lambda e, tt=tt: e.reduce_max(
                        STMX[:, tt:tt + 1], xres(tt), AX.X), drain=True)
                bld.mark("vector", "lnq")
            bld.op("vector", lambda e: e.tensor_scalar(MEAN[:, :], STAT[:, :], 1.0 / D, None, ALU.mult), drain=True)
            bld.op("vector", lambda e: e.tensor_scalar(VARB[:, :], STSQ[:, :], 1.0 / D, None, ALU.mult), drain=True)
            bld.op("vector", lambda e: e.tensor_tensor(STSQ[:, :], MEAN[:, :], MEAN[:, :], ALU.mult), drain=True)
            bld.op("vector", lambda e: e.tensor_sub(VARB[:, :], VARB[:, :], STSQ[:, :]), drain=True)
            bld.op("vector", lambda e: e.tensor_scalar(VARB[:, :], VARB[:, :], EPS_LN, None, ALU.add), drain=True)
            tv = bld.mark("vector", "lnq")
            bld.op("scalar", lambda e: e.activation(STSQ[:, :], VARB[:, :], ACT.Sqrt),
                   waits=[tv], drain=True)
            ts = bld.mark("scalar", "gq")
            bld.op("vector", lambda e: e.reciprocal(RSD[:, :], STSQ[:, :]), waits=[ts], drain=True)
            bld.op("vector", lambda e: e.tensor_tensor(NMR[:, :], MEAN[:, :], RSD[:, :], ALU.mult), drain=True)
            bld.op("vector", lambda e: e.tensor_scalar(NMR[:, :], NMR[:, :], -1.0, None, ALU.mult), drain=True)
            if need_max:
                bld.op("vector", lambda e: e.tensor_sub(LMX[:, :], STMX[:, :], MEAN[:, :]), drain=True)
                bld.op("vector", lambda e: e.tensor_tensor(LMX[:, :], LMX[:, :], RSD[:, :], ALU.mult), drain=True)
            return bld.mark("vector", "lnq")

        def global_max_ag(idx, ncols):
            bld.op("vector", lambda e, ncols=ncols: e.tensor_reduce(
                GM[:, 0:1], LMX[:, 0:ncols], AX.X, ALU.max), drain=True)
            tv = bld.mark("vector", "lnq")
            tp1 = bld.dma("sync", pm_d[:, :], GM[:, 0:1], waits=[tv])
            tp2 = bld.dma("sync", GVP[0:1, :], _ap(pm_d, 0, [(128, 1), (1, 128)]), waits=[tp1])
            bld.op("vector", lambda e: e.tensor_reduce(GS[0:1, 0:1], GVP[0:1, :], AX.X, ALU.max),
                   waits=[tp2], drain=True)
            tv1b = bld.mark("vector", "lnq")
            td = bld.dma("sync", qm_in[idx][:, :], GS[0:1, 0:1], waits=[tv1b])
            tc = bld.op("gpsimd", lambda e: e.collective_compute(
                "AllGather", ALU.bypass, replica_groups=[list(range(NCORE))],
                ins=[qm_in[idx].ap()], outs=[qm_out[idx].ap()]), sem="cc", waits=[td])
            tl = bld.dma("sync", GVP[0:1, 0:8], _ap(qm_out[idx], 0, [(8, 1), (1, 8)]), waits=[tc])
            bld.op("vector", lambda e: e.tensor_reduce(GS[0:1, 0:1], GVP[0:1, 0:8], AX.X, ALU.max),
                   waits=[tl], drain=True)
            bld.op("vector", lambda e: e.tensor_scalar(GS[0:1, 0:1], GS[0:1, 0:1],
                                                       EPS_Q, 1.0 / 255.0, ALU.max, ALU.mult), drain=True)
            bld.op("vector", lambda e: e.reciprocal(RQ[0:1, 0:1], GS[0:1, 0:1]), drain=True)
            tv2 = bld.mark("vector", "lnq")
            tb1 = bld.dma("sync", bc_d[0:1, 0:1], GS[0:1, 0:1], waits=[tv2])
            tb2 = bld.dma("sync", bc_d[0:1, 1:2], RQ[0:1, 0:1], waits=[tb1])
            tb3 = bld.dma("sync", SQB[:, 0:1], _ap(bc_d, 0, [(0, 128), (1, 1)]), waits=[tb2])
            tb4 = bld.dma("sync", RQB[:, 0:1], _ap(bc_d, 1, [(0, 128), (1, 1)]), waits=[tb3])
            return tb4

        def attn_a():
            ln_stats(False)
            bld.barrier()
            if AA_STAGE < 1:
                return
            for tt in range(16 if os.environ.get("MK_NOLN", "0") != "1" else 0):
                sl = QTB[:, 0:1024] if tt % 2 == 0 else QTB[:, 1024:2048]
                w = [("tpv", bld.cnt["tpv"] - 8)] if tt >= 2 else ()
                bld.op("scalar", lambda e, tt=tt, sl=sl: e.activation(
                    sl, xres(tt), ACT.Identity, bias=NMR[:, tt:tt + 1], scale=RSD[:, tt:tt + 1]),
                    waits=w, drain=True)
                ts = bld.mark("scalar", "gq")
                tp_tiles([(sl[:, 128 * dt:128 * (dt + 1)],
                           ar2(2048 * dt + 128 * tt, 128), [ts])
                          for dt in range(8)])
            bld.barrier()
            if AA_STAGE < 2:
                return
            psz = [PA, PU0]
            psh = [PB, PU1]
            ep1_0 = bld.cnt["ep1"]
            vev_0 = bld.cnt["vev"]
            zi = 0
            for ct in range(8):
                tw1 = bld.dma("sync", WZC[:, :],
                              _ap(wzT_d, 128 * ct, [(1024, 128), (131072, 8), (1, 128)]),
                              waits=[("mm1", bld.cnt["mm1"])])
                tw2 = bld.dma("sync", WHC[:, :],
                              _ap(whT_d, 128 * ct, [(1024, 128), (131072, 8), (1, 128)]))
                for nq in range(4):
                    par = zi % 2
                    wts = [tw1, tw2]
                    if zi >= 2:
                        wts.append(("ep1", ep1_0 + 2 * zi - 2))
                    for kt in range(8):
                        bld.op("tensor", (lambda e: e.nop()) if NOMM else (lambda e, kt=kt, nq=nq, par=par: e.matmul(
                            psz[par][:, 0:512], WZC[:, 128 * kt:128 * (kt + 1)],
                            ar2(2048 * kt + 512 * nq, 512),
                            start=(kt == 0), stop=(kt == 7))),
                            waits=wts if kt == 0 else ())
                    for kt in range(8):
                        tmm = bld.op("tensor", (lambda e: e.nop()) if NOMM else (lambda e, kt=kt, nq=nq, par=par: e.matmul(
                            psh[par][:, 0:512], WHC[:, 128 * kt:128 * (kt + 1)],
                            ar2(2048 * kt + 512 * nq, 512),
                            start=(kt == 0), stop=(kt == 7))),
                            sem="mm1" if kt == 7 else None)
                    if AA2 < 1:
                        bld.mark("scalar", "ep1")
                        bld.mark("scalar", "ep1")
                        bld.mark("vector", "vev")
                        zi += 1
                        continue
                    sw = [tmm]
                    if zi >= 2 and AA2 >= 3:
                        sw.append(("vev", vev_0 + zi - 1))
                    bld.op("vector", lambda e, par=par: e.tensor_copy(
                        SCR[:, 512 * par:512 * (par + 1)], psz[par][:, 0:512]),
                        waits=sw, drain=True)
                    bld.op("vector", lambda e, par=par: e.tensor_copy(
                        SCR[:, 1024 + 512 * par:1024 + 512 * (par + 1)], psh[par][:, 0:512]),
                        drain=True)
                    tve0 = bld.mark("vector", "ep1")
                    bld.op("scalar", lambda e, ct=ct, par=par: e.activation(
                        QTB[:, 1024 * par:1024 * par + 512], SCR[:, 512 * par:512 * (par + 1)], SIGF,
                        bias=BZS[:, ct:ct + 1], scale=1.0), waits=[tve0], drain=True)
                    bld.op("scalar", lambda e, ct=ct, par=par: e.activation(
                        QTB[:, 1024 * par + 512:1024 * par + 1024],
                        SCR[:, 1024 + 512 * par:1024 + 512 * (par + 1)], ACT.Identity,
                        bias=BHS[:, ct:ct + 1], scale=1.0), drain=True)
                    tsc = bld.mark("scalar", "ep1")
                    if AA2 < 3:
                        bld.mark("vector", "vev")
                        zi += 1
                        continue
                    bld.op("vector", lambda e, par=par, ct=ct, nq=nq: e.tensor_scalar(
                        AR1[:, 2048 * ct + 512 * nq: 2048 * ct + 512 * (nq + 1)],
                        QTB[:, 1024 * par:1024 * par + 512], -1.0, 1.0, ALU.mult, ALU.add),
                        waits=[tsc], drain=True)
                    bld.op("vector", lambda e, ct=ct, nq=nq: e.tensor_scalar(
                        AR1[:, 2048 * ct + 512 * nq: 2048 * ct + 512 * (nq + 1)],
                        AR1[:, 2048 * ct + 512 * nq: 2048 * ct + 512 * (nq + 1)],
                        1e-8, None, ALU.add), drain=True)
                    bld.op("vector", lambda e, par=par, ct=ct, nq=nq: e.tensor_tensor(
                        AR1[:, 16384 + 2048 * ct + 512 * nq: 16384 + 2048 * ct + 512 * (nq + 1)],
                        QTB[:, 1024 * par:1024 * par + 512], QTB[:, 1024 * par + 512:1024 * par + 1024],
                        ALU.mult), drain=True)
                    bld.mark("vector", "vev")
                    zi += 1
            bld.barrier()
            if AA_STAGE < 3:
                return
            for ct in range(8):
                a_ap = AR1[:, 2048 * ct:2048 * (ct + 1)]
                b_ap = AR1[:, 16384 + 2048 * ct:16384 + 2048 * (ct + 1)]
                hs = CHN[:, 0:2048]
                bld.op("vector", lambda e, a_ap=a_ap, b_ap=b_ap, hs=hs: e.tensor_tensor_scan(
                    hs, a_ap, b_ap, 0.0, ALU.mult, ALU.add), drain=True)
                bld.op("vector", lambda e, hs=hs, ct=ct: e.tensor_copy(
                    CAB[:, ct:ct + 1], hs[:, 2047:2048]), drain=True)
                tv = bld.mark("vector", "lnq")
                bld.op("scalar", lambda e, a_ap=a_ap: e.activation(
                    SCR[:, 0:2048], a_ap, ACT.Ln),
                    waits=[("lnq", bld.cnt["lnq"])], drain=True)
                ts = bld.mark("scalar", "gq")
                bld.op("vector", lambda e: e.reduce_sum(GM[:, 0:1], SCR[:, 0:2048], AX.X),
                       waits=[ts], drain=True)
                tv2 = bld.mark("vector", "lnq")
                bld.op("scalar", lambda e, ct=ct: e.activation(
                    CAB[:, 8 + ct:9 + ct], GM[:, 0:1], ACT.Exp), waits=[tv2], drain=True)
            bld.mark("scalar", "gq")
            bld.barrier()
            if AA_STAGE < 4:
                return
            td = bld.dma("sync", ca_in[:, :], CAB[:, :])
            tc = bld.op("gpsimd", lambda e: e.collective_compute(
                "AllGather", ALU.bypass, replica_groups=[list(range(NCORE))],
                ins=[ca_in.ap()], outs=[ca_out.ap()]), sem="cc", waits=[td])
            tfull = bld.dma("sync", SCR[:, 0:128].rearrange("p (c j) -> p c j", c=8),
                            ca_out[:, :].rearrange("(c p) j -> p c j", c=8, p=128),
                            waits=[tc])
            bld.op("vector", lambda e: e.tensor_scalar(
                CAG[:, :], SCR[:, 0:64], GMSK[:, 0:1], None, ALU.mult),
                waits=[tfull, ("d_sync", 1)], drain=True)
            bld.op("vector", lambda e: e.scalar_tensor_tensor(
                CAG[:, :], SCR[:, 64:128], GMSK[:, 1:2], CAG[:, :], ALU.mult, ALU.add),
                drain=True)
            tl = bld.mark("vector", "lnq")
            for ct in range(8):
                bld.op("vector", lambda e: e.memset(CCB[:, 0:1], 0.0),
                       waits=[tl] if ct == 0 else (), drain=True)
                bld.op("vector", lambda e, ct=ct: e.tensor_copy(
                    CCB[:, 1:2], CAG[:, ct:ct + 1]), drain=True)
                for k in (2, 3):
                    bld.op("vector", lambda e, ct=ct, k=k: e.tensor_tensor(
                        CCT[:, 0:1], CAG[:, (k - 1) * 16 + 8 + ct:(k - 1) * 16 + 9 + ct],
                        CCB[:, k - 1:k], ALU.mult), drain=True)
                    bld.op("vector", lambda e, ct=ct, k=k: e.tensor_tensor(
                        CCB[:, k:k + 1], CAG[:, (k - 1) * 16 + ct:(k - 1) * 16 + ct + 1],
                        CCT[:, 0:1], ALU.add), drain=True)
                bld.op("vector", lambda e: e.tensor_tensor(CCB[:, :], CCB[:, :], PMSK[:, :], ALU.mult), drain=True)
                bld.op("vector", lambda e, ct=ct: e.reduce_sum(CARR[:, ct:ct + 1], CCB[:, :], AX.X), drain=True)
            if AA_STAGE < 5:
                return
            for ct in range(8):
                a_ap = AR1[:, 2048 * ct:2048 * (ct + 1)]
                b_ap = AR1[:, 16384 + 2048 * ct:16384 + 2048 * (ct + 1)]
                w = [("tpv", bld.cnt["tpv"] - 16)] if ct >= 1 else ()
                bld.op("vector", lambda e, a_ap=a_ap, b_ap=b_ap, ct=ct: e.tensor_tensor_scan(
                    CHN[:, 0:2048], a_ap, b_ap, CARR[:, ct:ct + 1], ALU.mult, ALU.add),
                    waits=w, drain=True)
                tv = bld.mark("vector", "lnq")
                tp_tiles([(CHN[:, 128 * tt:128 * (tt + 1)],
                           ar2(1024 * tt + 128 * ct, 128), [tv])
                          for tt in range(16)])
            bld.barrier()
            if AA_STAGE < 6:
                return
            tpv = ("tpv", bld.cnt["tpv"])
            for tt in range(16):
                bld.op("vector", lambda e, tt=tt: e.tensor_tensor(
                    xres(tt), xres(tt), ar2(1024 * tt, 1024), ALU.add),
                    waits=[tpv] if tt == 0 else (), drain=True)
                bld.mark("vector", "rad")
            bld.barrier()

        def ffn(ci):
            ln_stats(True)
            global_max_ag(2 * ci, 16)
            bld.barrier()
            bld.op("vector", lambda e: e.tensor_scalar(SCQ[:, :], RSD[:, :], RQB[:, 0:1], None, ALU.mult), drain=True)
            bld.op("vector", lambda e: e.tensor_scalar(BSQ[:, :], NMR[:, :], RQB[:, 0:1], None, ALU.mult), drain=True)
            bld.op("vector", lambda e: e.tensor_scalar(SC1[:, :], SW1[:, :], SQB[:, 0:1], None, ALU.mult), drain=True)
            bld.op("vector", lambda e: e.reciprocal(RBQ[:, :], SC1[:, :]), drain=True)
            bld.op("vector", lambda e: e.tensor_tensor(QBI[:, :], B1V[:, :], RBQ[:, :], ALU.mult), drain=True)
            bld.op("vector", lambda e: e.tensor_copy(QTI[:, 0:32], QBI[:, :]), drain=True)
            bld.op("vector", lambda e: e.tensor_tensor(B1Q[:, :], QTI[:, 0:32], SC1[:, :], ALU.mult), drain=True)
            bld.mark("vector", "lnq")
            for tt in range(16):
                par = tt % 2
                qi = QTI[:, 1024 * par:1024 * (par + 1)]
                qb = QTB[:, 0:1024] if par == 0 else QTB[:, 1024:2048]
                w = [("tpv", bld.cnt["tpv"] - 8)] if tt >= 2 else ()
                bld.op("vector", lambda e, tt=tt, qi=qi: e.tensor_scalar(
                    qi, xres(tt), SCQ[:, tt:tt + 1], BSQ[:, tt:tt + 1], ALU.mult, ALU.add),
                    waits=w, drain=True)
                bld.op("vector", lambda e, qi=qi, qb=qb: e.tensor_scalar(
                    qb, qi, 0.0, None, ALU.max), drain=True)
                tv = bld.mark("vector", "lnq")
                tp_tiles([(qb[:, 128 * dt:128 * (dt + 1)],
                           ar2(2048 * dt + 128 * tt, 128), [tv])
                          for dt in range(8)])
            for kt in range(8):
                bld.dma("sync", ar1(4096 * kt, 4096),
                        qw1T_d[128 * kt:128 * (kt + 1), :])
            bld.op("vector", lambda e: e.memset(MAXA[:, :], 0.0), drain=True)
            bld.barrier()
            pp = [PA, PB]
            mm1_0 = bld.cnt["mm1"]
            ep1_0 = bld.cnt["ep1"]
            vev_0 = bld.cnt["vev"]
            sp_toks = {}
            for idx in range(128):
                ft, nq = divmod(idx, 4)
                par = idx % 2
                wts = []
                if idx >= 2:
                    wts.append(("ep1", ep1_0 + idx - 1))
                for kt in range(8):
                    tmm = bld.op("tensor", lambda e, kt=kt, ft=ft, nq=nq, par=par: e.matmul(
                        pp[par][:, 0:512],
                        AR1[:, 4096 * kt + 128 * ft: 4096 * kt + 128 * (ft + 1)],
                        AR2[:, 2048 * kt + 512 * nq: 2048 * kt + 512 * (nq + 1)],
                        start=(kt == 0), stop=(kt == 7)),
                        sem="mm1" if kt == 7 else None,
                        waits=wts if kt == 0 else ())
                sw = [tmm]
                if idx >= 2:
                    sw.append(sp_toks[idx - 2])
                    sw.append(("vev", vev_0 + idx - 1))
                bld.op("scalar", lambda e, ft=ft, par=par: e.activation(
                    HSB[:, 512 * par:512 * (par + 1)], pp[par][:, 0:512], ACT.Relu,
                    bias=B1Q[:, ft:ft + 1], scale=SC1[:, ft:ft + 1]),
                    waits=sw, drain=True)
                tep = bld.mark("scalar", "ep1")
                sp_toks[idx] = bld.dma("sync", h_d[128 * ft:128 * (ft + 1), 512 * nq:512 * (nq + 1)],
                                       HSB[:, 512 * par:512 * (par + 1)], waits=[tep],
                                       chan=f"h{par}")
                bld.op("vector", lambda e, par=par: e.tensor_tensor(
                    MAXA[:, :], MAXA[:, :], HSB[:, 512 * par:512 * (par + 1)], ALU.max),
                    waits=[tep], drain=True)
                bld.mark("vector", "vev")
            bld.barrier()
            bld.op("vector", lambda e: e.tensor_reduce(LMX[:, 0:1], MAXA[:, :], AX.X, ALU.max), drain=True)
            bld.mark("vector", "lnq")
            global_max_ag(2 * ci + 1, 1)
            bld.barrier()
            bld.op("vector", lambda e: e.tensor_scalar(SC2[:, :], SW2[:, :], SQB[:, 0:1], None, ALU.mult), drain=True)
            bld.op("vector", lambda e: e.reciprocal(RBQ[:, 0:8], SC2[:, :]), drain=True)
            bld.op("vector", lambda e: e.tensor_tensor(QBI[:, 0:8], B2V[:, :], RBQ[:, 0:8], ALU.mult), drain=True)
            bld.op("vector", lambda e: e.tensor_copy(QTI[:, 0:8], QBI[:, 0:8]), drain=True)
            bld.op("vector", lambda e: e.tensor_tensor(B2Q[:, :], QTI[:, 0:8], SC2[:, :], ALU.mult), drain=True)
            bld.mark("vector", "lnq")
            for kt in range(32):
                bld.dma("scalar", ar1(1024 * kt, 1024),
                        qw2T_d[128 * kt:128 * (kt + 1), :])
            bld.barrier()
            NSLOT = 8
            psall = [PA, PB, PU0, PU1, PEa, PEb]
            mm2_0 = bld.cnt["mm2"]
            ep2_0 = bld.cnt["ep2"]
            q2_toks = {}
            for nq in range(4):
                for kt in range(32):
                    gi = nq * 32 + kt
                    slot = gi % NSLOT
                    par = gi % 2
                    wts = []
                    if gi >= NSLOT:
                        wts.append(("mm2", mm2_0 + gi - NSLOT + 1))
                    if gi >= 2:
                        wts.append(q2_toks[gi - 2])
                    tld = bld.dma("scalar", QTB[:, 512 * par:512 * (par + 1)],
                                  h_d[128 * kt:128 * (kt + 1), 512 * nq:512 * (nq + 1)],
                                  waits=wts, chan=f"m{par}")
                    bld.op("vector", lambda e, par=par: e.tensor_scalar(
                        QTI[:, 512 * par:512 * (par + 1)], QTB[:, 512 * par:512 * (par + 1)],
                        RQB[:, 0:1], None, ALU.mult), waits=[tld], drain=True)
                    bld.op("vector", lambda e, par=par, slot=slot: e.tensor_copy(
                        AR2[:, 512 * slot:512 * (slot + 1)], QTI[:, 512 * par:512 * (par + 1)]),
                        drain=True)
                    q2_toks[gi] = bld.mark("vector", "q2b")
                for kt in range(32):
                    gi = nq * 32 + kt
                    slot = gi % NSLOT
                    for mt in range(8):
                        w = []
                        if mt == 0:
                            w.append(q2_toks[gi])
                        if kt == 0 and nq > 0:
                            w.append(("ep2", ep2_0 + 8 * nq))
                        src = psall[mt][:, 0:512] if mt < 6 else PWD[:, 512 * (mt - 6):512 * (mt - 5)]
                        bld.op("tensor", lambda e, kt=kt, mt=mt, slot=slot, src=src: e.matmul(
                            src,
                            AR1[:, 1024 * kt + 128 * mt: 1024 * kt + 128 * (mt + 1)],
                            AR2[:, 512 * slot:512 * (slot + 1)],
                            start=(kt == 0), stop=(kt == 31), skip_group_check=True),
                            sem="mm2" if mt == 7 else None,
                            waits=w)
                for mt in range(8):
                    src = psall[mt][:, 0:512] if mt < 6 else PWD[:, 512 * (mt - 6):512 * (mt - 5)]
                    par = mt % 2
                    w = [("mm2", bld.cnt["mm2"]), ("d_sync", 1)]
                    bld.op("scalar", lambda e, mt=mt, src=src, par=par: e.activation(
                        HSB[:, 512 * par:512 * (par + 1)], src, ACT.Identity,
                        bias=B2Q[:, mt:mt + 1], scale=SC2[:, mt:mt + 1]),
                        waits=w, drain=True)
                    tep = bld.mark("scalar", "ep2")
                    bld.dma("sync", o2_d[128 * mt:128 * (mt + 1), 512 * nq:512 * (nq + 1)],
                            HSB[:, 512 * par:512 * (par + 1)], waits=[tep])
            bld.barrier()
            for tt in range(16):
                par = tt % 2
                sl = CHN[:, 0:1024] if par == 0 else CHN[:, 1024:2048]
                yt = QTB[:, 0:1024] if par == 0 else QTB[:, 1024:2048]
                w = [("rad", bld.cnt["rad"] - 1)] if tt >= 2 else ()
                tl = bld.dma("sync", sl,
                             _ap(o2_d, 128 * tt, [(TLOC, 128), (128 * TLOC, 8), (1, 128)]),
                             waits=[("tpv", bld.cnt["tpv"] - 8)] if tt >= 2 else ())
                tpv = tp_tiles([(sl[:, 128 * dt:128 * (dt + 1)],
                                 yt[:, 128 * dt:128 * (dt + 1)], [tl] + list(w))
                                for dt in range(8)])
                bld.op("vector", lambda e, tt=tt, yt=yt: e.tensor_tensor(
                    xres(tt), xres(tt), yt, ALU.add), waits=[tpv], drain=True)
                bld.mark("vector", "rad")
            bld.barrier()

        def attn_b():
            ln_stats(False)
            bld.barrier()
            for tt in range(16):
                sl = QTB[:, 0:1024] if tt % 2 == 0 else QTB[:, 1024:2048]
                w = [("tpv", bld.cnt["tpv"] - 8), ("d_sync", 1)] if tt >= 2 else ()
                bld.op("scalar", lambda e, tt=tt, sl=sl: e.activation(
                    sl, xres(tt), ACT.Identity, bias=NMR[:, tt:tt + 1], scale=RSD[:, tt:tt + 1]),
                    waits=w, drain=True)
                ts = bld.mark("scalar", "gq")
                bld.dma("sync", _ap(xtok_in, 131072 * tt, [(1024, 128), (1, 1024)]), sl, waits=[ts])
                tp_tiles([(sl[:, 128 * dt:128 * (dt + 1)],
                           ar2(2048 * dt + 128 * tt, 128), [ts])
                          for dt in range(8)])
            tpv = ("tpv", bld.cnt["tpv"])
            for dtile in range(8):
                bld.dma("sync", xt_in[128 * dtile:128 * (dtile + 1), :],
                        ar2(2048 * dtile, 2048), waits=[tpv] if dtile == 0 else ())
            bld.op("gpsimd", lambda e: e.collective_compute(
                "AllGather", ALU.bypass, replica_groups=[list(range(NCORE))],
                ins=[xtok_in.ap()], outs=[xtok_all.ap()]), sem="cc",
                waits=[("d_sync", bld.cnt["d_sync"])])
            bld.op("gpsimd", lambda e: e.collective_compute(
                "AllGather", ALU.bypass, replica_groups=[list(range(NCORE))],
                ins=[xt_in.ap()], outs=[xt_all.ap()]), sem="cc")
            bld.barrier()
            ph = bld.getpid("sync", "prl")

            def chn(jj, k):
                return CHN[:, (jj * 8 + k) * 128:(jj * 8 + k) * 128 + 128]
            psj = [PA, PB, PU0, PU1]

            def ld_all(e, half, ph=ph):
                pid = ph["sv"]
                R = (pid // 2) * 1024 + (4096 if half else 0)
                c0 = (pid % 2) * 1024
                src = bass.AP(tensor=xt_all, offset=R * TLOC + c0,
                              ap=[[TLOC, 128], [128 * TLOC, 8], [1, 1024]])
                papl = AR1.ap()
                dst = bass.AP(tensor=papl.tensor, offset=8192 * half,
                              ap=[[papl.ap[0][0], 128], [1024, 8], [1, 1024]])
                return e.dma_start(out=dst, in_=src)
            ta = bld.op("sync", lambda e: ld_all(e, 0), sem="d_sync", amt=16)
            tb = bld.op("sync", lambda e: ld_all(e, 1), sem="d_sync", amt=16)
            for ci in range(16):
                gi, jj = divmod(ci, 4)
                for dt in range(8):
                    for hf in range(2):
                        w = [ta, tb] if (ci == 0 and dt == 0 and hf == 0) else ()
                        bld.op("vector", lambda e, ci=ci, gi=gi, jj=jj, dt=dt, hf=hf: e.tensor_copy(
                            AR1[:, 16384 + 1024 * ci + 128 * dt + 64 * hf:
                                16384 + 1024 * ci + 128 * dt + 64 * hf + 64],
                            AR1[:, 8192 * hf + 1024 * dt + 256 * gi + 64 * jj:
                                8192 * hf + 1024 * dt + 256 * gi + 64 * jj + 64]),
                            waits=w, drain=True)
            trp = bld.mark("vector", "vev")

            def xtc(ci, dt):
                return AR1[:, 16384 + 1024 * ci + 128 * dt: 16384 + 1024 * ci + 128 * (dt + 1)]
            for gi in range(4):
                jtok = {}
                for jj in range(4):
                    for kt in range(8):
                        w = []
                        if kt == 0:
                            w.append(trp)
                            if gi > 0:
                                w.append(("vev", bld.cnt["vev"]))
                        jtok[jj] = bld.op("tensor", lambda e, jj=jj, kt=kt, gi=gi: e.matmul(
                            psj[jj][:, 0:128], xtc(4 * gi + jj, kt), xtc(4 * gi + jj, kt),
                            start=(kt == 0), stop=(kt == 7)),
                            sem="vmm" if kt == 7 else None,
                            waits=w)
                for jj in range(4):
                    bld.op("vector", lambda e, jj=jj: e.tensor_scalar(
                        chn(jj, 2), psj[jj][:, 0:128], NEGC[:, 0:1], None, ALU.mult),
                        waits=[jtok[jj]], drain=True)
                    jtok[jj] = bld.mark("vector", "vev")
                for jj in range(4):
                    for pq in (0, 64):
                        for fq in (0, 64):
                            bld.op("gpsimd", lambda e, jj=jj, pq=pq, fq=fq: e.affine_select(
                                chn(jj, 0)[pq:pq + 64, fq:fq + 64],
                                chn(jj, 2)[pq:pq + 64, fq:fq + 64],
                                [[-1, 64]], ALU.is_ge, 0.0, base=-1, channel_multiplier=1),
                                waits=[jtok[jj]] if (pq == 0 and fq == 0) else (), drain=True)
                            bld.op("gpsimd", lambda e, jj=jj, pq=pq, fq=fq: e.affine_select(
                                chn(jj, 1)[pq:pq + 64, fq:fq + 64],
                                chn(jj, 2)[pq:pq + 64, fq:fq + 64],
                                [[1, 64]], ALU.is_ge, 0.0, base=-1, channel_multiplier=-1),
                                drain=True)
                    jtok[jj] = bld.mark("gpsimd", "prl")
                for jj in range(4):
                    bld.op("vector", lambda e, jj=jj: e.tensor_tensor(
                        chn(jj, 6), chn(jj, 1), IDN[:, :], ALU.add),
                        waits=[jtok[jj]], drain=True)
                    jtok[jj] = bld.mark("vector", "vev")
                pcur = {jj: 0 for jj in range(4)}
                qcur = {jj: 1 for jj in range(4)}
                tcur = {jj: 6 for jj in range(4)}
                for k in range(1, 6):
                    for jj in range(4):
                        pn = 2 if pcur[jj] in (0, 3) else 3
                        tk = bld.op("tensor", lambda e, jj=jj, p=pcur[jj], q=qcur[jj]: e.matmul(
                            psj[jj][:, 0:128], chn(jj, q), chn(jj, p),
                            start=True, stop=True), sem="vmm", waits=[jtok[jj]])
                        bld.op("vector", lambda e, jj=jj, pn=pn: e.tensor_copy(
                            chn(jj, pn), psj[jj][:, 0:128]), waits=[tk], drain=True)
                        jtok[jj] = bld.mark("vector", "vev")
                    if k < 5:
                        for jj in range(4):
                            qn = 4 if qcur[jj] in (1, 5) else 5
                            tk = bld.op("tensor", lambda e, jj=jj, p=pcur[jj], q=qcur[jj]: e.matmul(
                                psj[jj][:, 0:128], chn(jj, p), chn(jj, q),
                                start=True, stop=True), sem="vmm", waits=[jtok[jj]])
                            bld.op("vector", lambda e, jj=jj, qn=qn: e.tensor_copy(
                                chn(jj, qn), psj[jj][:, 0:128]), waits=[tk], drain=True)
                            jtok[jj] = bld.mark("vector", "vev")
                            qcur[jj] = qn
                    for jj in range(4):
                        pn = 2 if pcur[jj] in (0, 3) else 3
                        tC = tcur[jj]
                        tN = 13 - tC
                        tk = bld.op("tensor", lambda e, jj=jj, pn=pn, tC=tC: e.matmul(
                            psj[jj][:, 0:128], chn(jj, pn), chn(jj, tC),
                            start=True, stop=True), sem="vmm", waits=[jtok[jj]])
                        bld.op("vector", lambda e, jj=jj, tC=tC, tN=tN: e.tensor_tensor(
                            chn(jj, tN), chn(jj, tC), psj[jj][:, 0:128], ALU.add),
                            waits=[tk], drain=True)
                        jtok[jj] = bld.mark("vector", "vev")
                        tcur[jj] = tN
                        pcur[jj] = pn
                for jj in range(4):
                    bld.dma("sync", tt_in[128 * (4 * gi + jj):128 * (4 * gi + jj + 1), :],
                            chn(jj, tcur[jj]), waits=[jtok[jj]])
                bld.mark("gpsimd", "prl")
            bld.barrier()
            bld.op("gpsimd", lambda e: e.collective_compute(
                "AllGather", ALU.bypass, replica_groups=[list(range(NCORE))],
                ins=[tt_in.ap()], outs=[tt_out.ap()]), sem="cc")
            for dtile in range(8):
                twd = bld.op("tensor", lambda e, dtile=dtile: e.matmul(
                    PWD[:, 128 * dtile:128 * (dtile + 1)], IDN[:, :],
                    WD0[:, 128 * dtile:128 * (dtile + 1)],
                    start=(dtile % 4 == 0), stop=False,
                    skip_group_check=True), sem="wmm" if dtile == 7 else None)
            for dtile in range(8):
                if dtile < 4:
                    bld.op("scalar", lambda e, dtile=dtile: e.copy(
                        WDS[:, 128 * dtile:128 * (dtile + 1)],
                        PWD[:, 128 * dtile:128 * (dtile + 1)]),
                        waits=[twd] if dtile == 0 else (), drain=True)
                    bld.mark("scalar", "wrs")
                else:
                    bld.op("vector", lambda e, dtile=dtile: e.tensor_copy(
                        WDS[:, 128 * dtile:128 * (dtile + 1)],
                        PWD[:, 128 * dtile:128 * (dtile + 1)]),
                        waits=[twd] if dtile == 4 else (), drain=True)
                    bld.mark("vector", "wrv")
            bld.barrier()
            wrs0 = bld.cnt["wrs"]
            wrv0 = bld.cnt["wrv"]
            umm0 = bld.cnt["umm"]
            wmm0 = bld.cnt["wmm"]
            emm0 = bld.cnt["emm"]
            vmm0 = bld.cnt["vmm"]
            vev0 = bld.cnt["vev"]
            uev0 = bld.cnt["uev"]
            eev0 = bld.cnt["eev"]
            yev0 = bld.cnt["yev"]
            grp_toks = {}

            def load_group(g):
                par = g % 2
                base_xt = 4096 * par
                base_x = 8192 + 4096 * par
                wts = []
                if g >= 2:
                    nd = 4 * (g - 1)
                    wts = [("umm", umm0 + nd), ("wmm", wmm0 + nd),
                           ("vmm", vmm0 + nd), ("emm", emm0 + nd)]
                Rrow = 1024 * (g // 8)
                c0 = 64 * ((4 * g) % 32)
                cname = f"gl{par}"
                for jj in range(4):
                    for half in range(2):
                        src = _ap(xt_all, (Rrow + 4096 * half) * TLOC + c0 + 64 * jj,
                                  [(TLOC, 128), (128 * TLOC, 8), (1, 64)])
                        dsta = Bld.sb4(AR1, base_xt + 1024 * jj + 64 * half,
                                       [(128, 8), (1, 64)])
                        bld.dma("sync", dsta, src,
                                waits=wts if (jj == 0 and half == 0) else (), chan=cname)
                for half in range(2):
                    src = _ap(xtok_all, (8192 * half + 256 * g) * D,
                              [(D, 64), (64 * D, 4), (1, D)])
                    dsta = AR1[64 * half:64 * (half + 1), base_x:base_x + 4096].rearrange(
                        "p (jj d) -> p jj d", jj=4)
                    bld.dma("sync", dsta, src, chan=cname)
                src = _ap(tt_out, 512 * g * 128, [(128, 128), (16384, 4), (1, 128)])
                dsta = TTG[:, 512 * par:512 * (par + 1)].rearrange("p (jj t) -> p jj t", jj=4)
                tok = bld.dma("sync", dsta, src, chan=cname)
                grp_toks[g] = tok
                return tok

            load_group(0)
            load_group(1)
            pu = [PU0, PU1]
            pe = [PEa, PEb]
            pv = [PA, PB]
            for j in range(NCH):
                g, jj = divmod(j, 4)
                par = j % 2
                gpar = g % 2
                base_xt = 4096 * gpar
                base_x = 8192 + 4096 * gpar
                if jj == 0 and g + 2 < NGRP:
                    load_group(g + 2)
                tld = grp_toks[g]
                for kt in range(8):
                    w = []
                    if kt == 0:
                        w.append(tld)
                        if j >= 2:
                            w.append(("vev", vev0 + (j - 2) + 1))
                    bld.op("tensor", lambda e, kt=kt, jj=jj, par=par, base_xt=base_xt: e.matmul(
                        pv[par][:, 0:128],
                        AR1[:, base_xt + 1024 * jj + 128 * kt: base_xt + 1024 * jj + 128 * (kt + 1)],
                        WVT[:, 128 * kt:128 * (kt + 1)],
                        start=(kt == 0), stop=False), waits=w)
                tvm = bld.op("tensor", lambda e, par=par: e.matmul(
                    pv[par][:, 0:128], ONESROW[0:1, 0:128], BKVR[0:1, 0:128],
                    start=False, stop=True), sem="vmm")
                bld.op("vector", lambda e, par=par: e.tensor_copy(
                    VSB[:, 128 * par:128 * (par + 1)], pv[par][:, 0:128]),
                    waits=[tvm], drain=True)
                tve = bld.mark("vector", "vev")
                uw = [("wrs", wrs0 + j), ("wrv", wrv0 + j)]
                if j >= 2:
                    uw.append(("uev", uev0 + (j - 2) + 1))
                for kt in range(8):
                    bld.op("tensor", lambda e, kt=kt, jj=jj, par=par, base_xt=base_xt: e.matmul(
                        pu[par][:, 0:128],
                        AR1[:, base_xt + 1024 * jj + 128 * kt: base_xt + 1024 * jj + 128 * (kt + 1)],
                        WDS[:, 128 * kt:128 * (kt + 1)],
                        start=(kt == 0), stop=False), waits=uw if kt == 0 else ())
                tum = bld.op("tensor", lambda e, par=par: e.matmul(
                    pu[par][:, 0:128], ONESROW[0:1, 0:128], NBKVR[0:1, 0:128],
                    start=False, stop=True), sem="umm")
                bld.op("vector", lambda e, par=par: e.tensor_copy(
                    USB[:, 128 * par:128 * (par + 1)], pu[par][:, 0:128]),
                    waits=[tum], drain=True)
                tue = bld.mark("vector", "uev")
                ew = [tue]
                if j >= 2:
                    ew.append(("eev", eev0 + (j - 2) + 1))
                    ew.append(("yev", yev0 + (j - 2) + 1))
                tem = bld.op("tensor", lambda e, jj=jj, par=par, gpar=gpar: e.matmul(
                    pe[par][:, 0:128], TTG[:, 512 * gpar + 128 * jj: 512 * gpar + 128 * (jj + 1)],
                    USB[:, 128 * par:128 * (par + 1)], start=True, stop=True),
                    sem="emm", waits=ew)
                bld.op("vector", lambda e, par=par: e.tensor_scalar(
                    ENCS[:, 128 * par:128 * (par + 1)], pe[par][:, 0:128],
                    NEGC[:, 0:1], None, ALU.mult), waits=[tem], drain=True)
                tee = bld.mark("vector", "eev")
                bld.op("vector", lambda e, par=par: e.tensor_tensor(
                    YSB[:, 128 * par:128 * (par + 1)], pe[par][:, 0:128],
                    VSB[:, 128 * par:128 * (par + 1)], ALU.add), waits=[tve], drain=True)
                tye = bld.mark("vector", "yev")
                bld.dma("scalar", ystage[CH * j:CH * (j + 1), :],
                        YSB[0:64, 128 * par:128 * (par + 1)], waits=[tye])
                bld.dma("scalar", ystage[S + CH * j:S + CH * (j + 1), :],
                        YSB[64:128, 128 * par:128 * (par + 1)])
                if j == 0 and os.environ.get("MK_DBG", "0") == "1":
                    bld.dma("scalar", dbg_d[:, 0:128], W0T[:, 0:128], waits=[tye, tee])
                    bld.dma("scalar", dbg_d[:, 128:256], ENCS[:, 0:128])
                    bld.dma("scalar", dbg_d[:, 256:384], VSB[:, 0:128])
                    bld.dma("scalar", dbg_d[:, 384:512], WD0[:, 0:128])
                    bld.dma("scalar", dbg_d[:, 512:640], TTG[:, 0:128])
                    bld.dma("scalar", dbg_d[:, 640:768], WVT[:, 0:128])
                    bld.dma("scalar", dbg_d[:, 768:896], WDS[:, 0:128])
                    bld.dma("gpsimd", dbg_d[:, 896:897], NEGC[:, 0:1], allow_slow_non_contiguous=True)
                    bld.op("vector", lambda e: e.engine_nop(),
                           waits=[("d_scalar", 1), ("d_gpsimd", 1)], drain=True)
                    bld.op("tensor", lambda e: e.engine_nop() if False else e.nop(),
                           waits=[("d_scalar", 1)])
                for dtile in range(8):
                    twm = bld.op("tensor", lambda e, dtile=dtile, jj=jj, par=par, base_x=base_x: e.matmul(
                        PWD[:, 128 * dtile:128 * (dtile + 1)],
                        AR1[:, base_x + jj * 1024 + 128 * dtile: base_x + jj * 1024 + 128 * (dtile + 1)],
                        ENCS[:, 128 * par:128 * (par + 1)],
                        start=False, stop=(j == NCH - 1), skip_group_check=True),
                        sem="wmm" if dtile == 7 else None,
                        waits=[tee] if dtile == 0 else ())
                for dtile in range(4):
                    bld.op("scalar", lambda e, dtile=dtile: e.copy(
                        WDS[:, 128 * dtile:128 * (dtile + 1)],
                        PWD[:, 128 * dtile:128 * (dtile + 1)]),
                        waits=[twm] if dtile == 0 else ())
                bld.mark("scalar", "wrs")
                for dtile in range(4, 8):
                    bld.op("vector", lambda e, dtile=dtile: e.tensor_copy(
                        WDS[:, 128 * dtile:128 * (dtile + 1)],
                        PWD[:, 128 * dtile:128 * (dtile + 1)]),
                        waits=[twm] if dtile == 4 else ())
                bld.mark("vector", "wrv")
            bld.barrier()
            bld.op("gpsimd", lambda e: e.collective_compute(
                "AllGather", ALU.bypass, replica_groups=[list(range(NCORE))],
                ins=[ystage.ap()], outs=[yall.ap()]), sem="cc")
            bld.barrier()
            phs = bld.getpid("sync", "ldy")
            pha = bld.getpid("scalar", "ldy")
            rad_y0 = bld.cnt["rad"]
            for i in range(8):
                eng = "sync" if i < 4 else "scalar"
                phx = phs if i < 4 else pha
                sl = SCR[:, 0:1024]

                def ld_y(e, i=i, phx=phx, sl=sl):
                    pid = phx["sv"]
                    src = bass.AP(tensor=yall, offset=pid * 262144 + 2097152 * i,
                                  ap=[[128, 128], [16384, 8], [1, 128]])
                    dsl = bass.AP(tensor=sl.tensor, offset=sl.offset,
                                  ap=[[sl.ap[0][0], 128], [128, 8], [1, 128]])
                    return e.dma_start(out=dsl, in_=src)

                def ld_y2(e, i=i, phx=phx, sl=sl):
                    pid = phx["sv"]
                    src = bass.AP(tensor=yall, offset=pid * 262144 + 2097152 * i + 8 * 16384,
                                  ap=[[128, 128], [16384, 8], [1, 128]])
                    dsl = bass.AP(tensor=sl.tensor, offset=sl.offset + 1024,
                                  ap=[[sl.ap[0][0], 128], [128, 8], [1, 128]])
                    return e.dma_start(out=dsl, in_=src)
                w = [("rad", rad_y0 + i)] if i >= 1 else ()
                t1 = bld.op(eng, ld_y, sem="d_" + eng, amt=16, waits=w)
                t2 = bld.op(eng, ld_y2, sem="d_" + eng, amt=16)
                xap1 = bass.AP(tensor=XRES.ap().tensor, offset=128 * i,
                               ap=[[XRES.ap().ap[0][0], 128], [1024, 8], [1, 128]])
                xap2 = bass.AP(tensor=XRES.ap().tensor, offset=8192 + 128 * i,
                               ap=[[XRES.ap().ap[0][0], 128], [1024, 8], [1, 128]])
                s1 = bass.AP(tensor=sl.tensor, offset=sl.offset,
                             ap=[[sl.ap[0][0], 128], [128, 8], [1, 128]])
                s2 = bass.AP(tensor=sl.tensor, offset=sl.offset + 1024,
                             ap=[[sl.ap[0][0], 128], [128, 8], [1, 128]])
                bld.op("vector", lambda e, xap1=xap1, s1=s1: e.tensor_tensor(
                    xap1, xap1, s1, ALU.add), waits=[t1, t2], drain=True)
                bld.op("vector", lambda e, xap2=xap2, s2=s2: e.tensor_tensor(
                    xap2, xap2, s2, ALU.add), drain=True)
                bld.mark("vector", "rad")
            bld.barrier()

        if N_SUB >= 1:
            attn_a()
        if N_SUB >= 2:
            ffn(0)
        if N_SUB >= 3:
            attn_b()
        if N_SUB >= 4:
            ffn(1)

        bld.dma("sync", _ap(out_d, 0, [(1024, 128), (131072, 16), (1, 1024)]), XRES[:, :])
        bld.op("sync", lambda e: e.wait_ge(bld.semh["d_sync"], bld.cnt["d_sync"]))
        bld.barrier()

        block = stack.enter_context(nc.Block())
        for eng in ENGS:
            def emit(e, eng=eng):
                for fn in bld.q[eng]:
                    fn(e)
            getattr(block, eng)(emit)
    return nc


def kernel(**inputs):
    x = np.ascontiguousarray(inputs["L0_attn_a_x"])
    Wz = np.ascontiguousarray(inputs["L0_attn_a_Wz"])
    bz = np.ascontiguousarray(inputs["L0_attn_a_bz"])
    Wh = np.ascontiguousarray(inputs["L0_attn_a_Wh"])
    bh = np.ascontiguousarray(inputs["L0_attn_a_bh"])
    W0 = np.ascontiguousarray(inputs["L0_attn_b_W0"])
    Wkv = np.ascontiguousarray(inputs["L0_attn_b_Wkv"])
    bkv = np.ascontiguousarray(inputs["L0_attn_b_bkv"])
    eta = np.asarray(inputs["L0_attn_b_eta"], np.float32).reshape(1, 1)
    w1 = np.ascontiguousarray(inputs["L0_ffn_w1"])
    b1 = np.ascontiguousarray(inputs["L0_ffn_b1"])
    w2 = np.ascontiguousarray(inputs["L0_ffn_w2"])
    b2 = np.ascontiguousarray(inputs["L0_ffn_b2"])

    bz2 = np.ascontiguousarray(bz.reshape(8, 128).T)
    bh2 = np.ascontiguousarray(bh.reshape(8, 128).T)
    b1_2 = np.ascontiguousarray(b1.reshape(32, 128).T)
    b2_2 = np.ascontiguousarray(b2.reshape(8, 128).T)

    nc = build_nc()
    in_maps = []
    for c in range(NCORE):
        b_, k_ = divmod(c, 4)
        xs = np.ascontiguousarray(x[b_, 2048 * k_:2048 * (k_ + 1), :])
        pm = np.zeros((128, 4), np.float32)
        pm[:, k_] = 1.0
        gm = np.zeros((128, 2), np.float32)
        gm[:, 0 if b_ == 0 else 1] = 1.0
        in_maps.append(dict(
            x_in=xs, wz_in=Wz, wh_in=Wh, bz_in=bz2, bh_in=bh2,
            w0s_in=np.ascontiguousarray(W0[128 * c:128 * (c + 1), :]),
            wvs_in=np.ascontiguousarray(Wkv[1024 + 128 * c:1024 + 128 * (c + 1), :]),
            bkv_in=np.ascontiguousarray(bkv[1024 + 128 * c:1024 + 128 * (c + 1)].reshape(1, 128)),
            eta_in=eta, w1_in=w1, w2_in=w2, b1_in=b1_2, b2_in=b2_2,
            pmask_in=pm, gmask_in=gm,
        ))
    res = run_bass_kernel_spmd(nc, in_maps, core_ids=list(range(NCORE)))
    out = np.empty((2, 8192, 1024), np.float32)
    for c in range(NCORE):
        b_, k_ = divmod(c, 4)
        out[b_, 2048 * k_:2048 * (k_ + 1), :] = res.results[c]["out_d"]
    return out


# version 09
# speedup vs baseline: 1.0339x; 1.0339x over previous
import os
import numpy as np
import concourse.bass as bass
import concourse.mybir as mybir
import bass_rust
from concourse.bass_utils import run_bass_kernel_spmd

F32 = mybir.dt.float32
BF16 = mybir.dt.bfloat16
I16 = mybir.dt.int16
ALU = mybir.AluOpType
ACT = mybir.ActivationFunctionType
AX = mybir.AxisListType
ROMAX = bass_rust.ReduceOp.max

NCORE = 8
MK_DRAIN = os.environ.get("MK_DRAIN", "1") == "1"
D = 1024
F = 4096
TLOC = 2048
B = 2
S = 8192
CH = 64
NCH = 128
NGRP = 32
EPS_LN = 1e-5
EPS_Q = 1e-8

ENGS = ("sync", "vector", "scalar", "tensor", "gpsimd")

N_SUB = int(os.environ.get("MK_SUBLAYERS", "4"))
AA_STAGE = int(os.environ.get("MK_AA_STAGE", "9"))
SIGF = ACT.Sigmoid if os.environ.get("MK_SIG", "1") == "1" else ACT.Identity
AA2 = int(os.environ.get("MK_AA2", "3"))
NOMM = os.environ.get("MK_NOMM", "0") == "1"


def _ap(handle, offset, dims):
    return bass.AP(tensor=handle, offset=offset, ap=[[s, c] for (s, c) in dims])


class Bld:
    def __init__(self, nc):
        self.nc = nc
        self.q = {e: [] for e in ENGS}
        self.semh = {}
        self.cnt = {}
        self.nbar = 0
        self.pid = {}
        self.clean = {e: False for e in ENGS}

    def sem(self, name):
        if name not in self.cnt:
            self.cnt[name] = 0
        return name

    def add(self, eng, fn):
        self.q[eng].append(fn)

    def op(self, eng, fn, sem=None, amt=1, waits=(), drain=False):
        for w in waits:
            if w is None:
                continue
            s, v = w
            if s.startswith("d_"):
                v = self.cnt.get(s, 0)
            if v <= 0:
                continue
            self.sem(s)
            self.add(eng, lambda e, s=s, v=v: e.wait_ge(self.semh[s], v))
        if drain and MK_DRAIN and not self.clean[eng]:
            self.add(eng, lambda e: e.drain())
            self.clean[eng] = True
        if sem is not None:
            self.sem(sem)
            self.cnt[sem] += amt
            tok = (sem, self.cnt[sem])
            self.add(eng, lambda e, fn=fn, s=sem, a=amt: fn(e).then_inc(self.semh[s], a))
            self.clean[eng] = False
            return tok
        self.add(eng, fn)
        self.clean[eng] = False
        return None

    def dma(self, eng, out, in_, waits=(), chan=None, **kw):
        sem = ("ch_" + chan) if chan else ("d_" + eng)
        return self.op(eng, lambda e: e.dma_start(out=out, in_=in_, **kw),
                       sem=sem, amt=16, waits=waits)

    def dmat(self, eng, out, in_, waits=()):
        return self.op(eng, lambda e: e.dma_start_transpose(out, in_),
                       sem="d_" + eng, amt=16, waits=waits)

    def mark(self, eng, sem):
        tok = self.op(eng, lambda e: e.drain(), sem=sem)
        self.clean[eng] = True
        return tok

    def pemark(self, sem):
        return self.op("tensor", lambda e: e.nop(), sem=sem)

    def barrier(self):
        self.nbar += 1
        self.sem("bar")
        for eng in ENGS:
            dn = "d_" + eng
            if self.cnt.get(dn, 0) > 0:
                self.add(eng, lambda e, s=dn, v=self.cnt[dn]: e.wait_ge(self.semh[s], v))
            if eng == "sync":
                for cn in list(self.cnt.keys()):
                    if cn.startswith("ch_") and self.cnt[cn] > 0:
                        self.add(eng, lambda e, s=cn, v=self.cnt[cn]: e.wait_ge(self.semh[s], v))
            if eng == "gpsimd" and self.cnt.get("cc", 0) > 0:
                self.add(eng, lambda e, v=self.cnt["cc"]: e.wait_ge(self.semh["cc"], v))
            self.add(eng, lambda e: e.drain())
            self.add(eng, lambda e: e.sem_inc(self.semh["bar"], 1))
        self.cnt["bar"] += len(ENGS)
        tgt = self.cnt["bar"]
        for eng in ENGS:
            self.add(eng, lambda e, t=tgt: e.wait_ge(self.semh["bar"], t))
            self.clean[eng] = True

    @staticmethod
    def xtg_par(tensor_handle, gi, dt, jj):
        ap = tensor_handle.ap()
        pstep = ap.ap[0][0]
        return bass.AP(tensor=ap.tensor, offset=dt * 1024 + gi * 256 + 64 * jj,
                       ap=[[pstep, 128], [8192, 2], [1, 64]])

    @staticmethod
    def xtg_sl(tensor_handle, base, dt, jj):
        ap = tensor_handle.ap()
        pstep = ap.ap[0][0]
        return bass.AP(tensor=ap.tensor, offset=base + dt * 512 + 64 * jj,
                       ap=[[pstep, 128], [256, 2], [1, 64]])

    @staticmethod
    def sb4(tensor_handle, col0, dims):
        ap = tensor_handle.ap()
        pstep = ap.ap[0][0]
        return bass.AP(tensor=ap.tensor, offset=col0,
                       ap=[[pstep, 128]] + [[st, ct] for (st, ct) in dims])

    def getpid(self, eng, key=None):
        k = (eng, key)
        if k not in self.pid:
            holder = {}

            def mk(e, h=holder):
                h["sv"] = e.snap(self.nc.partition_id(), min_val=0, max_val=7)
            self.add(eng, mk)
            self.pid[k] = holder
        return self.pid[k]


def build_nc():
    nc = bass.Bass("TRN2", target_bir_lowering=False,
                   detect_race_conditions=os.environ.get("MK_RACE", "0") == "1")
    dt_ = nc.dram_tensor

    x_in = dt_("x_in", [TLOC, D], F32, kind="ExternalInput")
    wz_in = dt_("wz_in", [D, D], F32, kind="ExternalInput")
    wh_in = dt_("wh_in", [D, D], F32, kind="ExternalInput")
    bz_in = dt_("bz_in", [128, 8], F32, kind="ExternalInput")
    bh_in = dt_("bh_in", [128, 8], F32, kind="ExternalInput")
    w0s_in = dt_("w0s_in", [128, D], F32, kind="ExternalInput")
    wvs_in = dt_("wvs_in", [128, D], F32, kind="ExternalInput")
    bkv_in = dt_("bkv_in", [1, 128], F32, kind="ExternalInput")
    eta_in = dt_("eta_in", [1, 1], F32, kind="ExternalInput")
    w1_in = dt_("w1_in", [F, D], F32, kind="ExternalInput")
    w2_in = dt_("w2_in", [D, F], F32, kind="ExternalInput")
    b1_in = dt_("b1_in", [128, 32], F32, kind="ExternalInput")
    b2_in = dt_("b2_in", [128, 8], F32, kind="ExternalInput")
    pmask_in = dt_("pmask_in", [128, 4], F32, kind="ExternalInput")
    gmask_in = dt_("gmask_in", [128, 2], F32, kind="ExternalInput")

    out_d = dt_("out_d", [TLOC, D], F32, kind="ExternalOutput")

    wzT_d = dt_("wzT_d", [D, D], BF16)
    whT_d = dt_("whT_d", [D, D], BF16)
    qw1T_d = dt_("qw1T_d", [D, F], BF16)
    qw2T_d = dt_("qw2T_d", [F, D], BF16)
    h_d = dt_("h_d", [F, TLOC], BF16)
    o2_d = dt_("o2_d", [D, TLOC], BF16)
    ca_in = dt_("ca_in", [128, 16], F32)
    ca_out = dt_("ca_out", [128 * NCORE, 16], F32)
    qm_in = [dt_(f"qm_in{i}", [1, 1], F32) for i in range(4)]
    qm_out = [dt_(f"qm_out{i}", [NCORE, 1], F32) for i in range(4)]
    xtok_in = dt_("xtok_in", [TLOC, D], BF16)
    xtok_all = dt_("xtok_all", [TLOC * NCORE, D], BF16)
    xt_in = dt_("xt_in", [D, TLOC], BF16)
    xt_all = dt_("xt_all", [D * NCORE, TLOC], BF16)
    tt_in = dt_("tt_in", [16 * 128, 128], BF16)
    tt_out = dt_("tt_out", [NCH * 128, 128], BF16)
    ystage = dt_("ystage", [B * S, 128], F32)
    bc_d = dt_("bc_d", [1, 8], F32)
    dbg_d = dt_("dbg_d", [128, 1024], BF16)
    pm_d = dt_("pm_d", [128, 1], F32)
    yall = dt_("yall", [B * S * NCORE, 128], F32)

    bld = Bld(nc)
    from contextlib import ExitStack
    with ExitStack() as stack:
        sb = lambda name, shape, dt: stack.enter_context(nc.sbuf_tensor(name, shape, dt))
        XRES = sb("XRES", [128, 16 * 1024], F32)
        AR1 = sb("AR1", [128, 32768], BF16)
        AR2 = sb("AR2", [128, 16384], BF16)
        SCR = sb("SCR", [128, 2048], F32)
        W0T = sb("W0T", [128, D], BF16)
        WVT = sb("WVT", [128, D], BF16)
        WD0 = sb("WD0", [128, D], BF16)
        WDS = sb("WDS", [128, D], BF16)
        IDN = sb("IDN", [128, 128], BF16)
        ONESB = sb("ONESB", [128, 128], BF16)
        ONESROW = sb("ONESROW", [1, 128], BF16)
        BKVR = sb("BKVR", [1, 128], BF16)
        NBKVR = sb("NBKVR", [1, 128], BF16)
        NEGC = sb("NEGC", [128, 1], F32)
        ETA1 = sb("ETA1", [1, 1], F32)
        SW1 = sb("SW1", [128, 32], F32)
        SW2 = sb("SW2", [128, 8], F32)
        SC1 = sb("SC1", [128, 32], F32)
        SC2 = sb("SC2", [128, 8], F32)
        RBQ = sb("RBQ", [128, 32], F32)
        QBI = sb("QBI", [128, 32], F32)
        B1Q = sb("B1Q", [128, 32], F32)
        B2Q = sb("B2Q", [128, 8], F32)
        B1V = sb("B1V", [128, 32], F32)
        B2V = sb("B2V", [128, 8], F32)
        BZS = sb("BZS", [128, 8], F32)
        BHS = sb("BHS", [128, 8], F32)
        PMSK = sb("PMSK", [128, 4], F32)
        STAT = sb("STAT", [128, 16], F32)
        STSQ = sb("STSQ", [128, 16], F32)
        STMX = sb("STMX", [128, 16], F32)
        MEAN = sb("MEAN", [128, 16], F32)
        VARB = sb("VARB", [128, 16], F32)
        RSD = sb("RSD", [128, 16], F32)
        NMR = sb("NMR", [128, 16], F32)
        LMX = sb("LMX", [128, 16], F32)
        SCQ = sb("SCQ", [128, 16], F32)
        BSQ = sb("BSQ", [128, 16], F32)
        GM = sb("GM", [128, 1], F32)
        GMA = sb("GMA", [128, 1], F32)
        GMW = sb("GMW", [128, 1], F32)
        GMB = sb("GMB", [128, 1], F32)
        GVP = sb("GVP", [1, 128], F32)
        GS = sb("GS", [1, 1], F32)
        RQ = sb("RQ", [1, 1], F32)
        RQB = sb("RQB", [128, 1], F32)
        SQB = sb("SQB", [128, 1], F32)
        QTI = sb("QTI", [128, 2 * 1024], I16)
        QTB = sb("QTB", [128, 2 * 1024], BF16)
        HSB = sb("HSB", [128, 2 * 512], BF16)
        MAXA = sb("MAXA", [128, 512], BF16)
        CAB = sb("CAB", [128, 16], F32)
        CAG = sb("CAG", [128, 64], F32)
        GMSK = sb("GMSK", [128, 2], F32)
        CARR = sb("CARR", [128, 8], F32)
        CCB = sb("CCB", [128, 4], F32)
        CCT = sb("CCT", [128, 1], F32)
        WZC = sb("WZC", [128, 1024], BF16)
        WHC = sb("WHC", [128, 1024], BF16)
        TTG = sb("TTG", [128, 2 * 512], BF16)
        VSB = sb("VSB", [128, 2 * 128], BF16)
        USB = sb("USB", [128, 2 * 128], BF16)
        ENCS = sb("ENCS", [128, 2 * 128], BF16)
        YSB = sb("YSB", [128, 2 * 128], F32)
        CHN = sb("CHN", [128, 4 * 8 * 128], BF16)
        PA = stack.enter_context(nc.psum_tensor("PA", [128, 512], F32))
        PB = stack.enter_context(nc.psum_tensor("PB", [128, 512], F32))
        PU0 = stack.enter_context(nc.psum_tensor("PU0", [128, 512], F32))
        PU1 = stack.enter_context(nc.psum_tensor("PU1", [128, 512], F32))
        PEa = stack.enter_context(nc.psum_tensor("PEa", [128, 512], F32))
        PEb = stack.enter_context(nc.psum_tensor("PEb", [128, 512], F32))
        PWD = stack.enter_context(nc.psum_tensor("PWD", [128, 1024], F32))

        for s in ["d_sync", "d_scalar", "d_gpsimd", "cc", "bar",
                  "mm1", "ep1", "vev", "mm2", "ep2", "q2b",
                  "vmm", "umm", "uev", "emm", "eev", "yev",
                  "wmm", "wrs", "wrv", "prl", "lnq", "sqv", "gq", "rad",
                  "ch_gl0", "ch_gl1", "ch_h0", "ch_h1", "ch_m0", "ch_m1", "tpm", "tpv"]:
            bld.sem(s)
        for name in list(bld.cnt.keys()):
            bld.semh[name] = stack.enter_context(nc.semaphore(name))

        def xres(tt):
            return XRES[:, 1024 * tt:1024 * (tt + 1)]

        def tp_tiles(items, banks=(PEa, PEb)):
            for i, (src, dst, w) in enumerate(items):
                bank = banks[i % 2]
                wts = list(w)
                if getattr(bld, "idnf_tok", None):
                    wts.append(bld.idnf_tok)
                wts.append(("tpv", bld.cnt["tpv"] - 1))
                tk = bld.op("tensor", lambda e, src=src, bank=bank: e.matmul(
                    bank[:, 0:128], src, IDN[:, :], start=True, stop=True),
                    sem="tpm", waits=wts)
                bld.op("vector", lambda e, dst=dst, bank=bank: e.tensor_copy(
                    dst, bank[:, 0:128]), waits=[tk], drain=True)
                bld.mark("vector", "tpv")
            return ("tpv", bld.cnt["tpv"])

        def ar1(c0, n):
            return AR1[:, c0:c0 + n]

        def ar2(c0, n):
            return AR2[:, c0:c0 + n]

        bld.dma("sync", XRES[:, :], _ap(x_in, 0, [(1024, 128), (131072, 16), (1, 1024)]))
        bld.dma("sync", BZS[:, :], bz_in[:, :])
        bld.dma("sync", BHS[:, :], bh_in[:, :])
        bld.dma("sync", PMSK[:, :], pmask_in[:, :])
        bld.dma("sync", GMSK[:, :], gmask_in[:, :])
        bld.dma("sync", B1V[:, :], b1_in[:, :])
        bld.dma("sync", B2V[:, :], b2_in[:, :])
        t_bkv = bld.dma("gpsimd", BKVR[0:1, :], bkv_in[:, :])
        t_eta = bld.dma("sync", ETA1[:, :], eta_in[:, :])

        bld.op("vector", lambda e: e.tensor_scalar(GS[0:1, 0:1], ETA1[0:1, 0:1],
               -1.0 / 1024.0, None, ALU.mult), waits=[t_eta], drain=True)
        bld.op("vector", lambda e: e.memset(ONESROW[:, :], 1.0), drain=True)
        bld.op("vector", lambda e: e.memset(ONESB[:, :], 1.0), drain=True)
        bld.op("vector", lambda e: e.tensor_scalar(NBKVR[0:1, :], BKVR[0:1, :],
               -1.0, None, ALU.mult), waits=[t_bkv], drain=True)
        tv0 = bld.mark("vector", "vev")
        tnc = bld.dma("sync", bc_d[0:1, 2:3], GS[0:1, 0:1], waits=[tv0])
        tnc2 = bld.dma("sync", NEGC[:, 0:1], _ap(bc_d, 2, [(0, 128), (1, 1)]), waits=[tnc])
        tvb = tnc2
        tvm = bld.mark("vector", "vev")
        bld.op("gpsimd", lambda e: e.affine_select(
            IDN[:, :], ONESB[:, :], [[-1, 128]], ALU.is_equal, 0.0,
            base=0, channel_multiplier=1), waits=[tvb, tvm], drain=True)
        bld.idnf_tok = bld.mark("gpsimd", "prl")

        for (w_in, wT_d) in [(wz_in, wzT_d), (wh_in, whT_d)]:
            for rt in range(8):
                sl = SCR[:, 0:1024] if rt % 2 == 0 else SCR[:, 1024:2048]
                w = [("sqv", bld.cnt["sqv"] - 1)] if (rt >= 2 or w_in is wh_in) else ()
                tl = bld.dma("sync", sl, w_in[128 * rt:128 * (rt + 1), :], waits=w)
                bld.op("vector", lambda e, sl=sl: e.tensor_copy(QTB[:, 0:1024], sl),
                       waits=[tl, ("tpv", bld.cnt["tpv"])], drain=True)
                tv = bld.mark("vector", "sqv")
                tpv = tp_tiles([(QTB[:, 128 * ct:128 * (ct + 1)],
                                 QTB[:, 1024 + 128 * ct:1024 + 128 * (ct + 1)],
                                 [tv, ("d_sync", 1)])
                                for ct in range(8)])
                bld.dma("sync", _ap(wT_d, 128 * rt, [(1024, 128), (131072, 8), (1, 128)]),
                        QTB[:, 1024:2048], waits=[tpv])
                bld.mark("vector", "sqv")
        for wi, (ws_in, WT) in enumerate([(w0s_in, W0T), (wvs_in, WVT)]):
            sl = SCR[:, 1024 * wi:1024 * (wi + 1)]
            qd = QTB[:, 1024 * wi:1024 * (wi + 1)]
            tl = bld.dma("sync", sl, ws_in[:, :],
                         waits=[("sqv", bld.cnt["sqv"])])
            bld.op("vector", lambda e, sl=sl, qd=qd: e.tensor_copy(qd, sl),
                   waits=[tl, ("tpv", bld.cnt["tpv"])], drain=True)
            tv = bld.mark("vector", "sqv")
            tp_tiles([(qd[:, 128 * dtile:128 * (dtile + 1)],
                       WT[:, 128 * dtile:128 * (dtile + 1)], [tv])
                      for dtile in range(8)])
            bld.mark("vector", "sqv")
        bld.op("vector", lambda e: e.tensor_sub(WD0[:, :], W0T[:, :], WVT[:, :]), drain=True)

        for ft in range(32):
            sl = SCR[:, 0:1024] if ft % 2 == 0 else SCR[:, 1024:2048]
            w = [("sqv", bld.cnt["sqv"] - 1)] if ft >= 2 else ()
            tl = bld.dma("sync", sl, w1_in[128 * ft:128 * (ft + 1), :], waits=w)
            bld.op("vector", lambda e, sl=sl: e.tensor_reduce(
                GM[:, 0:1], sl, AX.X, ALU.max, apply_absolute_value=True),
                waits=[tl], drain=True)
            bld.op("vector", lambda e, ft=ft: e.tensor_scalar(
                SW1[:, ft:ft + 1], GM[:, 0:1], EPS_Q, 1.0 / 127.0, ALU.max, ALU.mult),
                drain=True)
            bld.op("vector", lambda e: e.reciprocal(GMA[:, 0:1], GM[:, 0:1]), drain=True)
            bld.op("vector", lambda e: e.tensor_scalar(
                GMA[:, 0:1], GMA[:, 0:1], 127.0, None, ALU.mult), drain=True)
            qsl = QTI[:, 0:1024] if ft % 2 == 0 else QTI[:, 1024:2048]
            bsl = QTB[:, 0:1024] if ft % 2 == 0 else QTB[:, 1024:2048]
            bld.op("vector", lambda e, sl=sl, qsl=qsl: e.tensor_scalar(
                qsl, sl, GMA[:, 0:1], None, ALU.mult), drain=True)
            bld.op("vector", lambda e, qsl=qsl, bsl=bsl: e.tensor_copy(bsl, qsl),
                   waits=[("tpv", bld.cnt["tpv"] - 8)], drain=True)
            tq = bld.mark("vector", "sqv")
            tpv = tp_tiles([(bsl[:, 128 * kt:128 * (kt + 1)],
                             CHN[:, 128 * kt:128 * (kt + 1)], [tq, ("d_sync", 1)])
                            for kt in range(8)])
            bld.dma("sync", _ap(qw1T_d, 128 * ft, [(F, 128), (128 * F, 8), (1, 128)]),
                    CHN[:, 0:1024], waits=[tpv])
            bld.mark("vector", "sqv")
        for rt in range(8):
            for hh in range(2):
                w = [("gq", bld.cnt["gq"])] if (rt > 0 or hh > 0) else ()
                tl = bld.dma("sync", SCR[:, 0:2048],
                             w2_in[128 * rt:128 * (rt + 1), 2048 * hh:2048 * (hh + 1)], waits=w)
                if hh == 0:
                    bld.op("vector", lambda e: e.tensor_reduce(
                        GMW[:, 0:1], SCR[:, 0:2048], AX.X, ALU.max, apply_absolute_value=True),
                        waits=[tl], drain=True)
                else:
                    bld.op("vector", lambda e: e.tensor_reduce(
                        GMB[:, 0:1], SCR[:, 0:2048], AX.X, ALU.max, apply_absolute_value=True),
                        waits=[tl], drain=True)
                bld.mark("vector", "gq")
            bld.op("vector", lambda e: e.tensor_tensor(GMW[:, 0:1], GMW[:, 0:1], GMB[:, 0:1], ALU.max), drain=True)
            bld.op("vector", lambda e, rt=rt: e.tensor_scalar(
                SW2[:, rt:rt + 1], GMW[:, 0:1], EPS_Q, 1.0 / 127.0, ALU.max, ALU.mult),
                drain=True)
            bld.op("vector", lambda e: e.reciprocal(GMB[:, 0:1], GMW[:, 0:1]), drain=True)
            bld.op("vector", lambda e: e.tensor_scalar(
                GMB[:, 0:1], GMB[:, 0:1], 127.0, None, ALU.mult), drain=True)
            tvr = bld.mark("vector", "gq")
            for hh in range(2):
                tl = bld.dma("sync", SCR[:, 0:2048],
                             w2_in[128 * rt:128 * (rt + 1), 2048 * hh:2048 * (hh + 1)],
                             waits=[("gq", bld.cnt["gq"])])
                bld.op("vector", lambda e: e.tensor_scalar(
                    SCR[:, 0:2048], SCR[:, 0:2048], GMB[:, 0:1], None, ALU.mult),
                    waits=[tl, tvr], drain=True)
                tvq = bld.mark("vector", "gq")
                bld.op("gpsimd", lambda e: e.tensor_scalar(
                    QTI[:, 0:2048], SCR[:, 0:2048], 1.0, None, ALU.mult),
                    waits=[tvq], drain=True)
                bld.op("gpsimd", lambda e: e.tensor_copy(QTB[:, 0:2048], QTI[:, 0:2048]),
                       drain=True)
                tq = bld.mark("gpsimd", "gq")
                tpv = tp_tiles([(QTB[:, 128 * k:128 * (k + 1)],
                                 CHN[:, 128 * k:128 * (k + 1)], [tq, ("d_sync", 1)])
                                for k in range(16)])
                bld.dma("sync", _ap(qw2T_d, (2048 * hh * D) + 128 * rt,
                                    [(D, 128), (128 * D, 16), (1, 128)]),
                        CHN[:, 0:2048], waits=[tpv])
                bld.mark("gpsimd", "gq")
        bld.barrier()

        def ln_stats(need_max):
            for tt in range(16):
                sl = SCR[:, 0:1024] if tt % 2 == 0 else SCR[:, 1024:2048]
                w = [("lnq", bld.cnt["lnq"] - 1)] if tt >= 2 else ()
                bld.op("scalar", lambda e, tt=tt, sl=sl: e.activation(
                    sl, xres(tt), ACT.Square), waits=w, drain=True)
                tsq = bld.mark("scalar", "gq")
                bld.op("vector", lambda e, tt=tt: e.reduce_sum(
                    STAT[:, tt:tt + 1], xres(tt), AX.X), drain=True)
                bld.op("vector", lambda e, tt=tt, sl=sl: e.reduce_sum(
                    STSQ[:, tt:tt + 1], sl, AX.X), waits=[tsq], drain=True)
                if need_max:
                    bld.op("vector", lambda e, tt=tt: e.reduce_max(
                        STMX[:, tt:tt + 1], xres(tt), AX.X), drain=True)
                bld.mark("vector", "lnq")
            bld.op("vector", lambda e: e.tensor_scalar(MEAN[:, :], STAT[:, :], 1.0 / D, None, ALU.mult), drain=True)
            bld.op("vector", lambda e: e.tensor_scalar(VARB[:, :], STSQ[:, :], 1.0 / D, None, ALU.mult), drain=True)
            bld.op("vector", lambda e: e.tensor_tensor(STSQ[:, :], MEAN[:, :], MEAN[:, :], ALU.mult), drain=True)
            bld.op("vector", lambda e: e.tensor_sub(VARB[:, :], VARB[:, :], STSQ[:, :]), drain=True)
            bld.op("vector", lambda e: e.tensor_scalar(VARB[:, :], VARB[:, :], EPS_LN, None, ALU.add), drain=True)
            tv = bld.mark("vector", "lnq")
            bld.op("scalar", lambda e: e.activation(STSQ[:, :], VARB[:, :], ACT.Sqrt),
                   waits=[tv], drain=True)
            ts = bld.mark("scalar", "gq")
            bld.op("vector", lambda e: e.reciprocal(RSD[:, :], STSQ[:, :]), waits=[ts], drain=True)
            bld.op("vector", lambda e: e.tensor_tensor(NMR[:, :], MEAN[:, :], RSD[:, :], ALU.mult), drain=True)
            bld.op("vector", lambda e: e.tensor_scalar(NMR[:, :], NMR[:, :], -1.0, None, ALU.mult), drain=True)
            if need_max:
                bld.op("vector", lambda e: e.tensor_sub(LMX[:, :], STMX[:, :], MEAN[:, :]), drain=True)
                bld.op("vector", lambda e: e.tensor_tensor(LMX[:, :], LMX[:, :], RSD[:, :], ALU.mult), drain=True)
            return bld.mark("vector", "lnq")

        def global_max_ag(idx, ncols):
            bld.op("vector", lambda e, ncols=ncols: e.tensor_reduce(
                GM[:, 0:1], LMX[:, 0:ncols], AX.X, ALU.max), drain=True)
            tv = bld.mark("vector", "lnq")
            tp1 = bld.dma("sync", pm_d[:, :], GM[:, 0:1], waits=[tv])
            tp2 = bld.dma("sync", GVP[0:1, :], _ap(pm_d, 0, [(128, 1), (1, 128)]), waits=[tp1])
            bld.op("vector", lambda e: e.tensor_reduce(GS[0:1, 0:1], GVP[0:1, :], AX.X, ALU.max),
                   waits=[tp2], drain=True)
            tv1b = bld.mark("vector", "lnq")
            td = bld.dma("sync", qm_in[idx][:, :], GS[0:1, 0:1], waits=[tv1b])
            tc = bld.op("gpsimd", lambda e: e.collective_compute(
                "AllGather", ALU.bypass, replica_groups=[list(range(NCORE))],
                ins=[qm_in[idx].ap()], outs=[qm_out[idx].ap()]), sem="cc", waits=[td])
            tl = bld.dma("sync", GVP[0:1, 0:8], _ap(qm_out[idx], 0, [(8, 1), (1, 8)]), waits=[tc])
            bld.op("vector", lambda e: e.tensor_reduce(GS[0:1, 0:1], GVP[0:1, 0:8], AX.X, ALU.max),
                   waits=[tl], drain=True)
            bld.op("vector", lambda e: e.tensor_scalar(GS[0:1, 0:1], GS[0:1, 0:1],
                                                       EPS_Q, 1.0 / 255.0, ALU.max, ALU.mult), drain=True)
            bld.op("vector", lambda e: e.reciprocal(RQ[0:1, 0:1], GS[0:1, 0:1]), drain=True)
            tv2 = bld.mark("vector", "lnq")
            tb1 = bld.dma("sync", bc_d[0:1, 0:1], GS[0:1, 0:1], waits=[tv2])
            tb2 = bld.dma("sync", bc_d[0:1, 1:2], RQ[0:1, 0:1], waits=[tb1])
            tb3 = bld.dma("sync", SQB[:, 0:1], _ap(bc_d, 0, [(0, 128), (1, 1)]), waits=[tb2])
            tb4 = bld.dma("sync", RQB[:, 0:1], _ap(bc_d, 1, [(0, 128), (1, 1)]), waits=[tb3])
            return tb4

        def attn_a():
            ln_stats(False)
            bld.barrier()
            if AA_STAGE < 1:
                return
            for tt in range(16 if os.environ.get("MK_NOLN", "0") != "1" else 0):
                sl = QTB[:, 0:1024] if tt % 2 == 0 else QTB[:, 1024:2048]
                w = [("tpv", bld.cnt["tpv"] - 8)] if tt >= 2 else ()
                bld.op("scalar", lambda e, tt=tt, sl=sl: e.activation(
                    sl, xres(tt), ACT.Identity, bias=NMR[:, tt:tt + 1], scale=RSD[:, tt:tt + 1]),
                    waits=w, drain=True)
                ts = bld.mark("scalar", "gq")
                tp_tiles([(sl[:, 128 * dt:128 * (dt + 1)],
                           ar2(2048 * dt + 128 * tt, 128), [ts])
                          for dt in range(8)])
            bld.barrier()
            if AA_STAGE < 2:
                return
            psz = [PA, PU0]
            psh = [PB, PU1]
            ep1_0 = bld.cnt["ep1"]
            vev_0 = bld.cnt["vev"]
            zi = 0
            for ct in range(8):
                tw1 = bld.dma("sync", WZC[:, :],
                              _ap(wzT_d, 128 * ct, [(1024, 128), (131072, 8), (1, 128)]),
                              waits=[("mm1", bld.cnt["mm1"])])
                tw2 = bld.dma("sync", WHC[:, :],
                              _ap(whT_d, 128 * ct, [(1024, 128), (131072, 8), (1, 128)]))
                for nq in range(4):
                    par = zi % 2
                    wts = [tw1, tw2]
                    if zi >= 2:
                        wts.append(("ep1", ep1_0 + 2 * zi - 2))
                    for kt in range(8):
                        bld.op("tensor", (lambda e: e.nop()) if NOMM else (lambda e, kt=kt, nq=nq, par=par: e.matmul(
                            psz[par][:, 0:512], WZC[:, 128 * kt:128 * (kt + 1)],
                            ar2(2048 * kt + 512 * nq, 512),
                            start=(kt == 0), stop=(kt == 7))),
                            waits=wts if kt == 0 else ())
                    for kt in range(8):
                        tmm = bld.op("tensor", (lambda e: e.nop()) if NOMM else (lambda e, kt=kt, nq=nq, par=par: e.matmul(
                            psh[par][:, 0:512], WHC[:, 128 * kt:128 * (kt + 1)],
                            ar2(2048 * kt + 512 * nq, 512),
                            start=(kt == 0), stop=(kt == 7))),
                            sem="mm1" if kt == 7 else None)
                    if AA2 < 1:
                        bld.mark("scalar", "ep1")
                        bld.mark("scalar", "ep1")
                        bld.mark("vector", "vev")
                        zi += 1
                        continue
                    sw = [tmm]
                    if zi >= 2 and AA2 >= 3:
                        sw.append(("vev", vev_0 + zi - 1))
                    bld.op("vector", lambda e, par=par: e.tensor_copy(
                        SCR[:, 512 * par:512 * (par + 1)], psz[par][:, 0:512]),
                        waits=sw, drain=True)
                    bld.op("vector", lambda e, par=par: e.tensor_copy(
                        SCR[:, 1024 + 512 * par:1024 + 512 * (par + 1)], psh[par][:, 0:512]),
                        drain=True)
                    tve0 = bld.mark("vector", "ep1")
                    bld.op("scalar", lambda e, ct=ct, par=par: e.activation(
                        QTB[:, 1024 * par:1024 * par + 512], SCR[:, 512 * par:512 * (par + 1)], SIGF,
                        bias=BZS[:, ct:ct + 1], scale=1.0), waits=[tve0], drain=True)
                    bld.op("scalar", lambda e, ct=ct, par=par: e.activation(
                        QTB[:, 1024 * par + 512:1024 * par + 1024],
                        SCR[:, 1024 + 512 * par:1024 + 512 * (par + 1)], ACT.Identity,
                        bias=BHS[:, ct:ct + 1], scale=1.0), drain=True)
                    tsc = bld.mark("scalar", "ep1")
                    if AA2 < 3:
                        bld.mark("vector", "vev")
                        zi += 1
                        continue
                    bld.op("vector", lambda e, par=par, ct=ct, nq=nq: e.tensor_scalar(
                        AR1[:, 2048 * ct + 512 * nq: 2048 * ct + 512 * (nq + 1)],
                        QTB[:, 1024 * par:1024 * par + 512], -1.0, 1.0, ALU.mult, ALU.add),
                        waits=[tsc], drain=True)
                    bld.op("vector", lambda e, ct=ct, nq=nq: e.tensor_scalar(
                        AR1[:, 2048 * ct + 512 * nq: 2048 * ct + 512 * (nq + 1)],
                        AR1[:, 2048 * ct + 512 * nq: 2048 * ct + 512 * (nq + 1)],
                        1e-8, None, ALU.add), drain=True)
                    bld.op("vector", lambda e, par=par, ct=ct, nq=nq: e.tensor_tensor(
                        AR1[:, 16384 + 2048 * ct + 512 * nq: 16384 + 2048 * ct + 512 * (nq + 1)],
                        QTB[:, 1024 * par:1024 * par + 512], QTB[:, 1024 * par + 512:1024 * par + 1024],
                        ALU.mult), drain=True)
                    bld.mark("vector", "vev")
                    zi += 1
            bld.barrier()
            if AA_STAGE < 3:
                return
            for ct in range(8):
                a_ap = AR1[:, 2048 * ct:2048 * (ct + 1)]
                b_ap = AR1[:, 16384 + 2048 * ct:16384 + 2048 * (ct + 1)]
                hs = CHN[:, 0:2048]
                bld.op("vector", lambda e, a_ap=a_ap, b_ap=b_ap, hs=hs: e.tensor_tensor_scan(
                    hs, a_ap, b_ap, 0.0, ALU.mult, ALU.add), drain=True)
                bld.op("vector", lambda e, hs=hs, ct=ct: e.tensor_copy(
                    CAB[:, ct:ct + 1], hs[:, 2047:2048]), drain=True)
                tv = bld.mark("vector", "lnq")
                bld.op("scalar", lambda e, a_ap=a_ap: e.activation(
                    SCR[:, 0:2048], a_ap, ACT.Ln),
                    waits=[("lnq", bld.cnt["lnq"])], drain=True)
                ts = bld.mark("scalar", "gq")
                bld.op("vector", lambda e: e.reduce_sum(GM[:, 0:1], SCR[:, 0:2048], AX.X),
                       waits=[ts], drain=True)
                tv2 = bld.mark("vector", "lnq")
                bld.op("scalar", lambda e, ct=ct: e.activation(
                    CAB[:, 8 + ct:9 + ct], GM[:, 0:1], ACT.Exp), waits=[tv2], drain=True)
            bld.mark("scalar", "gq")
            bld.barrier()
            if AA_STAGE < 4:
                return
            td = bld.dma("sync", ca_in[:, :], CAB[:, :])
            tc = bld.op("gpsimd", lambda e: e.collective_compute(
                "AllGather", ALU.bypass, replica_groups=[list(range(NCORE))],
                ins=[ca_in.ap()], outs=[ca_out.ap()]), sem="cc", waits=[td])
            tfull = bld.dma("sync", SCR[:, 0:128].rearrange("p (c j) -> p c j", c=8),
                            ca_out[:, :].rearrange("(c p) j -> p c j", c=8, p=128),
                            waits=[tc])
            bld.op("vector", lambda e: e.tensor_scalar(
                CAG[:, :], SCR[:, 0:64], GMSK[:, 0:1], None, ALU.mult),
                waits=[tfull, ("d_sync", 1)], drain=True)
            bld.op("vector", lambda e: e.scalar_tensor_tensor(
                CAG[:, :], SCR[:, 64:128], GMSK[:, 1:2], CAG[:, :], ALU.mult, ALU.add),
                drain=True)
            tl = bld.mark("vector", "lnq")
            for ct in range(8):
                bld.op("vector", lambda e: e.memset(CCB[:, 0:1], 0.0),
                       waits=[tl] if ct == 0 else (), drain=True)
                bld.op("vector", lambda e, ct=ct: e.tensor_copy(
                    CCB[:, 1:2], CAG[:, ct:ct + 1]), drain=True)
                for k in (2, 3):
                    bld.op("vector", lambda e, ct=ct, k=k: e.tensor_tensor(
                        CCT[:, 0:1], CAG[:, (k - 1) * 16 + 8 + ct:(k - 1) * 16 + 9 + ct],
                        CCB[:, k - 1:k], ALU.mult), drain=True)
                    bld.op("vector", lambda e, ct=ct, k=k: e.tensor_tensor(
                        CCB[:, k:k + 1], CAG[:, (k - 1) * 16 + ct:(k - 1) * 16 + ct + 1],
                        CCT[:, 0:1], ALU.add), drain=True)
                bld.op("vector", lambda e: e.tensor_tensor(CCB[:, :], CCB[:, :], PMSK[:, :], ALU.mult), drain=True)
                bld.op("vector", lambda e, ct=ct: e.reduce_sum(CARR[:, ct:ct + 1], CCB[:, :], AX.X), drain=True)
            if AA_STAGE < 5:
                return
            for ct in range(8):
                a_ap = AR1[:, 2048 * ct:2048 * (ct + 1)]
                b_ap = AR1[:, 16384 + 2048 * ct:16384 + 2048 * (ct + 1)]
                w = [("tpv", bld.cnt["tpv"] - 16)] if ct >= 1 else ()
                bld.op("vector", lambda e, a_ap=a_ap, b_ap=b_ap, ct=ct: e.tensor_tensor_scan(
                    CHN[:, 0:2048], a_ap, b_ap, CARR[:, ct:ct + 1], ALU.mult, ALU.add),
                    waits=w, drain=True)
                tv = bld.mark("vector", "lnq")
                tp_tiles([(CHN[:, 128 * tt:128 * (tt + 1)],
                           ar2(1024 * tt + 128 * ct, 128), [tv])
                          for tt in range(16)])
            bld.barrier()
            if AA_STAGE < 6:
                return
            tpv = ("tpv", bld.cnt["tpv"])
            for tt in range(16):
                bld.op("vector", lambda e, tt=tt: e.tensor_tensor(
                    xres(tt), xres(tt), ar2(1024 * tt, 1024), ALU.add),
                    waits=[tpv] if tt == 0 else (), drain=True)
                bld.mark("vector", "rad")
            bld.barrier()

        def ffn(ci):
            ln_stats(True)
            global_max_ag(2 * ci, 16)
            bld.barrier()
            bld.op("vector", lambda e: e.tensor_scalar(SCQ[:, :], RSD[:, :], RQB[:, 0:1], None, ALU.mult), drain=True)
            bld.op("vector", lambda e: e.tensor_scalar(BSQ[:, :], NMR[:, :], RQB[:, 0:1], None, ALU.mult), drain=True)
            bld.op("vector", lambda e: e.tensor_scalar(SC1[:, :], SW1[:, :], SQB[:, 0:1], None, ALU.mult), drain=True)
            bld.op("vector", lambda e: e.reciprocal(RBQ[:, :], SC1[:, :]), drain=True)
            bld.op("vector", lambda e: e.tensor_tensor(QBI[:, :], B1V[:, :], RBQ[:, :], ALU.mult), drain=True)
            bld.op("vector", lambda e: e.tensor_copy(QTI[:, 0:32], QBI[:, :]), drain=True)
            bld.op("vector", lambda e: e.tensor_tensor(B1Q[:, :], QTI[:, 0:32], SC1[:, :], ALU.mult), drain=True)
            bld.mark("vector", "lnq")
            for tt in range(16):
                par = tt % 2
                qi = QTI[:, 1024 * par:1024 * (par + 1)]
                qb = QTB[:, 0:1024] if par == 0 else QTB[:, 1024:2048]
                w = [("tpv", bld.cnt["tpv"] - 8)] if tt >= 2 else ()
                bld.op("vector", lambda e, tt=tt, qi=qi: e.tensor_scalar(
                    qi, xres(tt), SCQ[:, tt:tt + 1], BSQ[:, tt:tt + 1], ALU.mult, ALU.add),
                    waits=w, drain=True)
                bld.op("vector", lambda e, qi=qi, qb=qb: e.tensor_scalar(
                    qb, qi, 0.0, None, ALU.max), drain=True)
                tv = bld.mark("vector", "lnq")
                tp_tiles([(qb[:, 128 * dt:128 * (dt + 1)],
                           ar2(2048 * dt + 128 * tt, 128), [tv])
                          for dt in range(8)])
            for kt in range(8):
                bld.dma("sync", ar1(4096 * kt, 4096),
                        qw1T_d[128 * kt:128 * (kt + 1), :])
            bld.op("vector", lambda e: e.memset(MAXA[:, :], 0.0), drain=True)
            bld.barrier()
            pp = [PA, PB]
            mm1_0 = bld.cnt["mm1"]
            ep1_0 = bld.cnt["ep1"]
            vev_0 = bld.cnt["vev"]
            sp_toks = {}
            for idx in range(128):
                ft, nq = divmod(idx, 4)
                par = idx % 2
                wts = []
                if idx >= 2:
                    wts.append(("ep1", ep1_0 + idx - 1))
                for kt in range(8):
                    tmm = bld.op("tensor", lambda e, kt=kt, ft=ft, nq=nq, par=par: e.matmul(
                        pp[par][:, 0:512],
                        AR1[:, 4096 * kt + 128 * ft: 4096 * kt + 128 * (ft + 1)],
                        AR2[:, 2048 * kt + 512 * nq: 2048 * kt + 512 * (nq + 1)],
                        start=(kt == 0), stop=(kt == 7)),
                        sem="mm1" if kt == 7 else None,
                        waits=wts if kt == 0 else ())
                sw = [tmm]
                if idx >= 2:
                    sw.append(sp_toks[idx - 2])
                    sw.append(("vev", vev_0 + idx - 1))
                bld.op("scalar", lambda e, ft=ft, par=par: e.activation(
                    HSB[:, 512 * par:512 * (par + 1)], pp[par][:, 0:512], ACT.Relu,
                    bias=B1Q[:, ft:ft + 1], scale=SC1[:, ft:ft + 1]),
                    waits=sw, drain=True)
                tep = bld.mark("scalar", "ep1")
                sp_toks[idx] = bld.dma("sync", h_d[128 * ft:128 * (ft + 1), 512 * nq:512 * (nq + 1)],
                                       HSB[:, 512 * par:512 * (par + 1)], waits=[tep],
                                       chan=f"h{par}")
                bld.op("vector", lambda e, par=par: e.tensor_tensor(
                    MAXA[:, :], MAXA[:, :], HSB[:, 512 * par:512 * (par + 1)], ALU.max),
                    waits=[tep], drain=True)
                bld.mark("vector", "vev")
            bld.barrier()
            bld.op("vector", lambda e: e.tensor_reduce(LMX[:, 0:1], MAXA[:, :], AX.X, ALU.max), drain=True)
            bld.mark("vector", "lnq")
            global_max_ag(2 * ci + 1, 1)
            bld.barrier()
            bld.op("vector", lambda e: e.tensor_scalar(SC2[:, :], SW2[:, :], SQB[:, 0:1], None, ALU.mult), drain=True)
            bld.op("vector", lambda e: e.reciprocal(RBQ[:, 0:8], SC2[:, :]), drain=True)
            bld.op("vector", lambda e: e.tensor_tensor(QBI[:, 0:8], B2V[:, :], RBQ[:, 0:8], ALU.mult), drain=True)
            bld.op("vector", lambda e: e.tensor_copy(QTI[:, 0:8], QBI[:, 0:8]), drain=True)
            bld.op("vector", lambda e: e.tensor_tensor(B2Q[:, :], QTI[:, 0:8], SC2[:, :], ALU.mult), drain=True)
            bld.mark("vector", "lnq")
            for kt in range(32):
                bld.dma("scalar", ar1(1024 * kt, 1024),
                        qw2T_d[128 * kt:128 * (kt + 1), :])
            bld.barrier()
            NSLOT = 8
            psall = [PA, PB, PU0, PU1, PEa, PEb]
            mm2_0 = bld.cnt["mm2"]
            ep2_0 = bld.cnt["ep2"]
            q2_toks = {}
            for nq in range(4):
                for kt in range(32):
                    gi = nq * 32 + kt
                    slot = gi % NSLOT
                    par = gi % 2
                    wts = []
                    if gi >= NSLOT:
                        wts.append(("mm2", mm2_0 + gi - NSLOT + 1))
                    if gi >= 2:
                        wts.append(q2_toks[gi - 2])
                    tld = bld.dma("scalar", QTB[:, 512 * par:512 * (par + 1)],
                                  h_d[128 * kt:128 * (kt + 1), 512 * nq:512 * (nq + 1)],
                                  waits=wts, chan=f"m{par}")
                    bld.op("vector", lambda e, par=par: e.tensor_scalar(
                        QTI[:, 512 * par:512 * (par + 1)], QTB[:, 512 * par:512 * (par + 1)],
                        RQB[:, 0:1], None, ALU.mult), waits=[tld], drain=True)
                    bld.op("vector", lambda e, par=par, slot=slot: e.tensor_copy(
                        AR2[:, 512 * slot:512 * (slot + 1)], QTI[:, 512 * par:512 * (par + 1)]),
                        drain=True)
                    q2_toks[gi] = bld.mark("vector", "q2b")
                for kt in range(32):
                    gi = nq * 32 + kt
                    slot = gi % NSLOT
                    for mt in range(8):
                        w = []
                        if mt == 0:
                            w.append(q2_toks[gi])
                        if kt == 0 and nq > 0:
                            w.append(("ep2", ep2_0 + 8 * nq))
                        src = psall[mt][:, 0:512] if mt < 6 else PWD[:, 512 * (mt - 6):512 * (mt - 5)]
                        bld.op("tensor", lambda e, kt=kt, mt=mt, slot=slot, src=src: e.matmul(
                            src,
                            AR1[:, 1024 * kt + 128 * mt: 1024 * kt + 128 * (mt + 1)],
                            AR2[:, 512 * slot:512 * (slot + 1)],
                            start=(kt == 0), stop=(kt == 31), skip_group_check=True),
                            sem="mm2" if mt == 7 else None,
                            waits=w)
                for mt in range(8):
                    src = psall[mt][:, 0:512] if mt < 6 else PWD[:, 512 * (mt - 6):512 * (mt - 5)]
                    par = mt % 2
                    w = [("mm2", bld.cnt["mm2"]), ("d_sync", 1)]
                    bld.op("scalar", lambda e, mt=mt, src=src, par=par: e.activation(
                        HSB[:, 512 * par:512 * (par + 1)], src, ACT.Identity,
                        bias=B2Q[:, mt:mt + 1], scale=SC2[:, mt:mt + 1]),
                        waits=w, drain=True)
                    tep = bld.mark("scalar", "ep2")
                    bld.dma("sync", o2_d[128 * mt:128 * (mt + 1), 512 * nq:512 * (nq + 1)],
                            HSB[:, 512 * par:512 * (par + 1)], waits=[tep])
            bld.barrier()
            for tt in range(16):
                par = tt % 2
                sl = CHN[:, 0:1024] if par == 0 else CHN[:, 1024:2048]
                yt = QTB[:, 0:1024] if par == 0 else QTB[:, 1024:2048]
                w = [("rad", bld.cnt["rad"] - 1)] if tt >= 2 else ()
                tl = bld.dma("sync", sl,
                             _ap(o2_d, 128 * tt, [(TLOC, 128), (128 * TLOC, 8), (1, 128)]),
                             waits=[("tpv", bld.cnt["tpv"] - 8)] if tt >= 2 else ())
                tpv = tp_tiles([(sl[:, 128 * dt:128 * (dt + 1)],
                                 yt[:, 128 * dt:128 * (dt + 1)], [tl] + list(w))
                                for dt in range(8)])
                bld.op("vector", lambda e, tt=tt, yt=yt: e.tensor_tensor(
                    xres(tt), xres(tt), yt, ALU.add), waits=[tpv], drain=True)
                bld.mark("vector", "rad")
            bld.barrier()

        def attn_b():
            ln_stats(False)
            bld.barrier()
            for tt in range(16):
                sl = QTB[:, 0:1024] if tt % 2 == 0 else QTB[:, 1024:2048]
                w = [("tpv", bld.cnt["tpv"] - 8), ("d_sync", 1)] if tt >= 2 else ()
                bld.op("scalar", lambda e, tt=tt, sl=sl: e.activation(
                    sl, xres(tt), ACT.Identity, bias=NMR[:, tt:tt + 1], scale=RSD[:, tt:tt + 1]),
                    waits=w, drain=True)
                ts = bld.mark("scalar", "gq")
                bld.dma("sync", _ap(xtok_in, 131072 * tt, [(1024, 128), (1, 1024)]), sl, waits=[ts])
                tp_tiles([(sl[:, 128 * dt:128 * (dt + 1)],
                           ar2(2048 * dt + 128 * tt, 128), [ts])
                          for dt in range(8)])
            tpv = ("tpv", bld.cnt["tpv"])
            for dtile in range(8):
                bld.dma("sync", xt_in[128 * dtile:128 * (dtile + 1), :],
                        ar2(2048 * dtile, 2048), waits=[tpv] if dtile == 0 else ())
            bld.op("gpsimd", lambda e: e.collective_compute(
                "AllGather", ALU.bypass, replica_groups=[list(range(NCORE))],
                ins=[xtok_in.ap()], outs=[xtok_all.ap()]), sem="cc",
                waits=[("d_sync", bld.cnt["d_sync"])])
            bld.op("gpsimd", lambda e: e.collective_compute(
                "AllGather", ALU.bypass, replica_groups=[list(range(NCORE))],
                ins=[xt_in.ap()], outs=[xt_all.ap()]), sem="cc")
            bld.barrier()
            ph = bld.getpid("sync", "prl")

            def chn(jj, k):
                return CHN[:, (jj * 8 + k) * 128:(jj * 8 + k) * 128 + 128]
            psj = [PA, PB, PU0, PU1]

            def ld_all(e, half, ph=ph):
                pid = ph["sv"]
                R = (pid // 2) * 1024 + (4096 if half else 0)
                c0 = (pid % 2) * 1024
                src = bass.AP(tensor=xt_all, offset=R * TLOC + c0,
                              ap=[[TLOC, 128], [128 * TLOC, 8], [1, 1024]])
                papl = AR1.ap()
                dst = bass.AP(tensor=papl.tensor, offset=8192 * half,
                              ap=[[papl.ap[0][0], 128], [1024, 8], [1, 1024]])
                return e.dma_start(out=dst, in_=src)
            ta = bld.op("sync", lambda e: ld_all(e, 0), sem="d_sync", amt=16)
            tb = bld.op("sync", lambda e: ld_all(e, 1), sem="d_sync", amt=16)
            for ci in range(16):
                gi, jj = divmod(ci, 4)
                for dt in range(8):
                    for hf in range(2):
                        w = [ta, tb] if (ci == 0 and dt == 0 and hf == 0) else ()
                        bld.op("vector", lambda e, ci=ci, gi=gi, jj=jj, dt=dt, hf=hf: e.tensor_copy(
                            AR1[:, 16384 + 1024 * ci + 128 * dt + 64 * hf:
                                16384 + 1024 * ci + 128 * dt + 64 * hf + 64],
                            AR1[:, 8192 * hf + 1024 * dt + 256 * gi + 64 * jj:
                                8192 * hf + 1024 * dt + 256 * gi + 64 * jj + 64]),
                            waits=w, drain=True)
            trp = bld.mark("vector", "vev")

            def xtc(ci, dt):
                return AR1[:, 16384 + 1024 * ci + 128 * dt: 16384 + 1024 * ci + 128 * (dt + 1)]
            for gi in range(4):
                jtok = {}
                for jj in range(4):
                    for kt in range(8):
                        w = []
                        if kt == 0:
                            w.append(trp)
                            if gi > 0:
                                w.append(("vev", bld.cnt["vev"]))
                        jtok[jj] = bld.op("tensor", lambda e, jj=jj, kt=kt, gi=gi: e.matmul(
                            psj[jj][:, 0:128], xtc(4 * gi + jj, kt), xtc(4 * gi + jj, kt),
                            start=(kt == 0), stop=(kt == 7)),
                            sem="vmm" if kt == 7 else None,
                            waits=w)
                for jj in range(4):
                    bld.op("vector", lambda e, jj=jj: e.tensor_scalar(
                        chn(jj, 2), psj[jj][:, 0:128], NEGC[:, 0:1], None, ALU.mult),
                        waits=[jtok[jj]], drain=True)
                    jtok[jj] = bld.mark("vector", "vev")
                for jj in range(4):
                    for pq in (0, 64):
                        for fq in (0, 64):
                            bld.op("gpsimd", lambda e, jj=jj, pq=pq, fq=fq: e.affine_select(
                                chn(jj, 0)[pq:pq + 64, fq:fq + 64],
                                chn(jj, 2)[pq:pq + 64, fq:fq + 64],
                                [[-1, 64]], ALU.is_ge, 0.0, base=-1, channel_multiplier=1),
                                waits=[jtok[jj]] if (pq == 0 and fq == 0) else (), drain=True)
                            bld.op("gpsimd", lambda e, jj=jj, pq=pq, fq=fq: e.affine_select(
                                chn(jj, 1)[pq:pq + 64, fq:fq + 64],
                                chn(jj, 2)[pq:pq + 64, fq:fq + 64],
                                [[1, 64]], ALU.is_ge, 0.0, base=-1, channel_multiplier=-1),
                                drain=True)
                    jtok[jj] = bld.mark("gpsimd", "prl")
                for jj in range(4):
                    bld.op("vector", lambda e, jj=jj: e.tensor_tensor(
                        chn(jj, 6), chn(jj, 1), IDN[:, :], ALU.add),
                        waits=[jtok[jj]], drain=True)
                    jtok[jj] = bld.mark("vector", "vev")
                pcur = {jj: 0 for jj in range(4)}
                qcur = {jj: 1 for jj in range(4)}
                tcur = {jj: 6 for jj in range(4)}
                for k in range(1, 6):
                    for jj in range(4):
                        pn = 2 if pcur[jj] in (0, 3) else 3
                        tk = bld.op("tensor", lambda e, jj=jj, p=pcur[jj], q=qcur[jj]: e.matmul(
                            psj[jj][:, 0:128], chn(jj, q), chn(jj, p),
                            start=True, stop=True), sem="vmm", waits=[jtok[jj]])
                        bld.op("vector", lambda e, jj=jj, pn=pn: e.tensor_copy(
                            chn(jj, pn), psj[jj][:, 0:128]), waits=[tk], drain=True)
                        jtok[jj] = bld.mark("vector", "vev")
                    if k < 5:
                        for jj in range(4):
                            qn = 4 if qcur[jj] in (1, 5) else 5
                            tk = bld.op("tensor", lambda e, jj=jj, p=pcur[jj], q=qcur[jj]: e.matmul(
                                psj[jj][:, 0:128], chn(jj, p), chn(jj, q),
                                start=True, stop=True), sem="vmm", waits=[jtok[jj]])
                            bld.op("vector", lambda e, jj=jj, qn=qn: e.tensor_copy(
                                chn(jj, qn), psj[jj][:, 0:128]), waits=[tk], drain=True)
                            jtok[jj] = bld.mark("vector", "vev")
                            qcur[jj] = qn
                    for jj in range(4):
                        pn = 2 if pcur[jj] in (0, 3) else 3
                        tC = tcur[jj]
                        tN = 13 - tC
                        tk = bld.op("tensor", lambda e, jj=jj, pn=pn, tC=tC: e.matmul(
                            psj[jj][:, 0:128], chn(jj, pn), chn(jj, tC),
                            start=True, stop=True), sem="vmm", waits=[jtok[jj]])
                        bld.op("vector", lambda e, jj=jj, tC=tC, tN=tN: e.tensor_tensor(
                            chn(jj, tN), chn(jj, tC), psj[jj][:, 0:128], ALU.add),
                            waits=[tk], drain=True)
                        jtok[jj] = bld.mark("vector", "vev")
                        tcur[jj] = tN
                        pcur[jj] = pn
                for jj in range(4):
                    bld.dma("sync", tt_in[128 * (4 * gi + jj):128 * (4 * gi + jj + 1), :],
                            chn(jj, tcur[jj]), waits=[jtok[jj]])
                bld.mark("gpsimd", "prl")
            bld.barrier()
            bld.op("gpsimd", lambda e: e.collective_compute(
                "AllGather", ALU.bypass, replica_groups=[list(range(NCORE))],
                ins=[tt_in.ap()], outs=[tt_out.ap()]), sem="cc")
            for dtile in range(8):
                twd = bld.op("tensor", lambda e, dtile=dtile: e.matmul(
                    PWD[:, 128 * dtile:128 * (dtile + 1)], IDN[:, :],
                    WD0[:, 128 * dtile:128 * (dtile + 1)],
                    start=(dtile % 4 == 0), stop=False,
                    skip_group_check=True), sem="wmm" if dtile == 7 else None)
            for dtile in range(8):
                if dtile < 4:
                    bld.op("scalar", lambda e, dtile=dtile: e.copy(
                        WDS[:, 128 * dtile:128 * (dtile + 1)],
                        PWD[:, 128 * dtile:128 * (dtile + 1)]),
                        waits=[twd] if dtile == 0 else (), drain=True)
                    bld.mark("scalar", "wrs")
                else:
                    bld.op("vector", lambda e, dtile=dtile: e.tensor_copy(
                        WDS[:, 128 * dtile:128 * (dtile + 1)],
                        PWD[:, 128 * dtile:128 * (dtile + 1)]),
                        waits=[twd] if dtile == 4 else (), drain=True)
                    bld.mark("vector", "wrv")
            bld.barrier()
            wrs0 = bld.cnt["wrs"]
            wrv0 = bld.cnt["wrv"]
            umm0 = bld.cnt["umm"]
            wmm0 = bld.cnt["wmm"]
            emm0 = bld.cnt["emm"]
            vmm0 = bld.cnt["vmm"]
            vev0 = bld.cnt["vev"]
            uev0 = bld.cnt["uev"]
            eev0 = bld.cnt["eev"]
            yev0 = bld.cnt["yev"]
            grp_toks = {}

            def load_group(g):
                par = g % 2
                base_xt = 4096 * par
                base_x = 8192 + 4096 * par
                wts = []
                if g >= 2:
                    nd = 4 * (g - 1)
                    wts = [("umm", umm0 + nd), ("wmm", wmm0 + nd),
                           ("vmm", vmm0 + nd), ("emm", emm0 + nd)]
                Rrow = 1024 * (g // 8)
                c0 = 64 * ((4 * g) % 32)
                cname = f"gl{par}"
                for jj in range(4):
                    for half in range(2):
                        src = _ap(xt_all, (Rrow + 4096 * half) * TLOC + c0 + 64 * jj,
                                  [(TLOC, 128), (128 * TLOC, 8), (1, 64)])
                        dsta = Bld.sb4(AR1, base_xt + 1024 * jj + 64 * half,
                                       [(128, 8), (1, 64)])
                        bld.dma("sync", dsta, src,
                                waits=wts if (jj == 0 and half == 0) else (), chan=cname)
                for half in range(2):
                    src = _ap(xtok_all, (8192 * half + 256 * g) * D,
                              [(D, 64), (64 * D, 4), (1, D)])
                    dsta = AR1[64 * half:64 * (half + 1), base_x:base_x + 4096].rearrange(
                        "p (jj d) -> p jj d", jj=4)
                    bld.dma("sync", dsta, src, chan=cname)
                src = _ap(tt_out, 512 * g * 128, [(128, 128), (16384, 4), (1, 128)])
                dsta = TTG[:, 512 * par:512 * (par + 1)].rearrange("p (jj t) -> p jj t", jj=4)
                tok = bld.dma("sync", dsta, src, chan=cname)
                grp_toks[g] = tok
                return tok

            load_group(0)
            load_group(1)
            pu = [PU0, PU1]
            pe = [PEa, PEb]
            pv = [PA, PB]
            for j in range(NCH):
                g, jj = divmod(j, 4)
                par = j % 2
                gpar = g % 2
                base_xt = 4096 * gpar
                base_x = 8192 + 4096 * gpar
                if jj == 0 and g + 2 < NGRP:
                    load_group(g + 2)
                tld = grp_toks[g]
                for kt in range(8):
                    w = []
                    if kt == 0:
                        w.append(tld)
                        if j >= 2:
                            w.append(("vev", vev0 + (j - 2) + 1))
                    bld.op("tensor", lambda e, kt=kt, jj=jj, par=par, base_xt=base_xt: e.matmul(
                        pv[par][:, 0:128],
                        AR1[:, base_xt + 1024 * jj + 128 * kt: base_xt + 1024 * jj + 128 * (kt + 1)],
                        WVT[:, 128 * kt:128 * (kt + 1)],
                        start=(kt == 0), stop=False), waits=w)
                tvm = bld.op("tensor", lambda e, par=par: e.matmul(
                    pv[par][:, 0:128], ONESROW[0:1, 0:128], BKVR[0:1, 0:128],
                    start=False, stop=True), sem="vmm")
                bld.op("vector", lambda e, par=par: e.tensor_copy(
                    VSB[:, 128 * par:128 * (par + 1)], pv[par][:, 0:128]),
                    waits=[tvm], drain=True)
                tve = bld.mark("vector", "vev")
                uw = [("wrs", wrs0 + j), ("wrv", wrv0 + j)]
                if j >= 2:
                    uw.append(("uev", uev0 + (j - 2) + 1))
                for kt in range(8):
                    bld.op("tensor", lambda e, kt=kt, jj=jj, par=par, base_xt=base_xt: e.matmul(
                        pu[par][:, 0:128],
                        AR1[:, base_xt + 1024 * jj + 128 * kt: base_xt + 1024 * jj + 128 * (kt + 1)],
                        WDS[:, 128 * kt:128 * (kt + 1)],
                        start=(kt == 0), stop=False), waits=uw if kt == 0 else ())
                tum = bld.op("tensor", lambda e, par=par: e.matmul(
                    pu[par][:, 0:128], ONESROW[0:1, 0:128], NBKVR[0:1, 0:128],
                    start=False, stop=True), sem="umm")
                bld.op("vector", lambda e, par=par: e.tensor_copy(
                    USB[:, 128 * par:128 * (par + 1)], pu[par][:, 0:128]),
                    waits=[tum], drain=True)
                tue = bld.mark("vector", "uev")
                ew = [tue]
                if j >= 2:
                    ew.append(("eev", eev0 + (j - 2) + 1))
                    ew.append(("yev", yev0 + (j - 2) + 1))
                tem = bld.op("tensor", lambda e, jj=jj, par=par, gpar=gpar: e.matmul(
                    pe[par][:, 0:128], TTG[:, 512 * gpar + 128 * jj: 512 * gpar + 128 * (jj + 1)],
                    USB[:, 128 * par:128 * (par + 1)], start=True, stop=True),
                    sem="emm", waits=ew)
                bld.op("vector", lambda e, par=par: e.tensor_scalar(
                    ENCS[:, 128 * par:128 * (par + 1)], pe[par][:, 0:128],
                    NEGC[:, 0:1], None, ALU.mult), waits=[tem], drain=True)
                tee = bld.mark("vector", "eev")
                bld.op("vector", lambda e, par=par: e.tensor_tensor(
                    YSB[:, 128 * par:128 * (par + 1)], pe[par][:, 0:128],
                    VSB[:, 128 * par:128 * (par + 1)], ALU.add), waits=[tve], drain=True)
                tye = bld.mark("vector", "yev")
                bld.dma("scalar", ystage[CH * j:CH * (j + 1), :],
                        YSB[0:64, 128 * par:128 * (par + 1)], waits=[tye])
                bld.dma("scalar", ystage[S + CH * j:S + CH * (j + 1), :],
                        YSB[64:128, 128 * par:128 * (par + 1)])
                if j == 0 and os.environ.get("MK_DBG", "0") == "1":
                    bld.dma("scalar", dbg_d[:, 0:128], W0T[:, 0:128], waits=[tye, tee])
                    bld.dma("scalar", dbg_d[:, 128:256], ENCS[:, 0:128])
                    bld.dma("scalar", dbg_d[:, 256:384], VSB[:, 0:128])
                    bld.dma("scalar", dbg_d[:, 384:512], WD0[:, 0:128])
                    bld.dma("scalar", dbg_d[:, 512:640], TTG[:, 0:128])
                    bld.dma("scalar", dbg_d[:, 640:768], WVT[:, 0:128])
                    bld.dma("scalar", dbg_d[:, 768:896], WDS[:, 0:128])
                    bld.dma("gpsimd", dbg_d[:, 896:897], NEGC[:, 0:1], allow_slow_non_contiguous=True)
                    bld.op("vector", lambda e: e.engine_nop(),
                           waits=[("d_scalar", 1), ("d_gpsimd", 1)], drain=True)
                    bld.op("tensor", lambda e: e.engine_nop() if False else e.nop(),
                           waits=[("d_scalar", 1)])
                for dtile in range(8):
                    twm = bld.op("tensor", lambda e, dtile=dtile, jj=jj, par=par, base_x=base_x: e.matmul(
                        PWD[:, 128 * dtile:128 * (dtile + 1)],
                        AR1[:, base_x + jj * 1024 + 128 * dtile: base_x + jj * 1024 + 128 * (dtile + 1)],
                        ENCS[:, 128 * par:128 * (par + 1)],
                        start=False, stop=(j == NCH - 1), skip_group_check=True),
                        sem="wmm" if dtile == 7 else None,
                        waits=[tee] if dtile == 0 else ())
                for dtile in range(4):
                    bld.op("scalar", lambda e, dtile=dtile: e.copy(
                        WDS[:, 128 * dtile:128 * (dtile + 1)],
                        PWD[:, 128 * dtile:128 * (dtile + 1)]),
                        waits=[twm] if dtile == 0 else ())
                bld.mark("scalar", "wrs")
                for dtile in range(4, 8):
                    bld.op("vector", lambda e, dtile=dtile: e.tensor_copy(
                        WDS[:, 128 * dtile:128 * (dtile + 1)],
                        PWD[:, 128 * dtile:128 * (dtile + 1)]),
                        waits=[twm] if dtile == 4 else ())
                bld.mark("vector", "wrv")
            bld.barrier()
            bld.op("gpsimd", lambda e: e.collective_compute(
                "AllGather", ALU.bypass, replica_groups=[list(range(NCORE))],
                ins=[ystage.ap()], outs=[yall.ap()]), sem="cc")
            bld.barrier()
            phs = bld.getpid("sync", "ldy")
            pha = bld.getpid("scalar", "ldy")
            rad_y0 = bld.cnt["rad"]
            for i in range(8):
                eng = "sync" if i < 4 else "scalar"
                phx = phs if i < 4 else pha
                sl = SCR[:, 0:1024]

                def ld_y(e, i=i, phx=phx, sl=sl):
                    pid = phx["sv"]
                    src = bass.AP(tensor=yall, offset=pid * 262144 + 2097152 * i,
                                  ap=[[128, 128], [16384, 8], [1, 128]])
                    dsl = bass.AP(tensor=sl.tensor, offset=sl.offset,
                                  ap=[[sl.ap[0][0], 128], [128, 8], [1, 128]])
                    return e.dma_start(out=dsl, in_=src)

                def ld_y2(e, i=i, phx=phx, sl=sl):
                    pid = phx["sv"]
                    src = bass.AP(tensor=yall, offset=pid * 262144 + 2097152 * i + 8 * 16384,
                                  ap=[[128, 128], [16384, 8], [1, 128]])
                    dsl = bass.AP(tensor=sl.tensor, offset=sl.offset + 1024,
                                  ap=[[sl.ap[0][0], 128], [128, 8], [1, 128]])
                    return e.dma_start(out=dsl, in_=src)
                w = [("rad", rad_y0 + i)] if i >= 1 else ()
                t1 = bld.op(eng, ld_y, sem="d_" + eng, amt=16, waits=w)
                t2 = bld.op(eng, ld_y2, sem="d_" + eng, amt=16)
                xap1 = bass.AP(tensor=XRES.ap().tensor, offset=128 * i,
                               ap=[[XRES.ap().ap[0][0], 128], [1024, 8], [1, 128]])
                xap2 = bass.AP(tensor=XRES.ap().tensor, offset=8192 + 128 * i,
                               ap=[[XRES.ap().ap[0][0], 128], [1024, 8], [1, 128]])
                s1 = bass.AP(tensor=sl.tensor, offset=sl.offset,
                             ap=[[sl.ap[0][0], 128], [128, 8], [1, 128]])
                s2 = bass.AP(tensor=sl.tensor, offset=sl.offset + 1024,
                             ap=[[sl.ap[0][0], 128], [128, 8], [1, 128]])
                bld.op("vector", lambda e, xap1=xap1, s1=s1: e.tensor_tensor(
                    xap1, xap1, s1, ALU.add), waits=[t1, t2], drain=True)
                bld.op("vector", lambda e, xap2=xap2, s2=s2: e.tensor_tensor(
                    xap2, xap2, s2, ALU.add), drain=True)
                bld.mark("vector", "rad")
            bld.barrier()

        if N_SUB >= 1:
            attn_a()
        if N_SUB >= 2:
            ffn(0)
        if N_SUB >= 3:
            attn_b()
        if N_SUB >= 4:
            ffn(1)

        bld.dma("sync", _ap(out_d, 0, [(1024, 128), (131072, 16), (1, 1024)]), XRES[:, :])
        bld.op("sync", lambda e: e.wait_ge(bld.semh["d_sync"], bld.cnt["d_sync"]))
        bld.barrier()

        block = stack.enter_context(nc.Block())
        for eng in ENGS:
            def emit(e, eng=eng):
                for fn in bld.q[eng]:
                    fn(e)
            getattr(block, eng)(emit)
    return nc


def kernel(**inputs):
    x = np.ascontiguousarray(inputs["L0_attn_a_x"])
    Wz = np.ascontiguousarray(inputs["L0_attn_a_Wz"])
    bz = np.ascontiguousarray(inputs["L0_attn_a_bz"])
    Wh = np.ascontiguousarray(inputs["L0_attn_a_Wh"])
    bh = np.ascontiguousarray(inputs["L0_attn_a_bh"])
    W0 = np.ascontiguousarray(inputs["L0_attn_b_W0"])
    Wkv = np.ascontiguousarray(inputs["L0_attn_b_Wkv"])
    bkv = np.ascontiguousarray(inputs["L0_attn_b_bkv"])
    eta = np.asarray(inputs["L0_attn_b_eta"], np.float32).reshape(1, 1)
    w1 = np.ascontiguousarray(inputs["L0_ffn_w1"])
    b1 = np.ascontiguousarray(inputs["L0_ffn_b1"])
    w2 = np.ascontiguousarray(inputs["L0_ffn_w2"])
    b2 = np.ascontiguousarray(inputs["L0_ffn_b2"])

    bz2 = np.ascontiguousarray(bz.reshape(8, 128).T)
    bh2 = np.ascontiguousarray(bh.reshape(8, 128).T)
    b1_2 = np.ascontiguousarray(b1.reshape(32, 128).T)
    b2_2 = np.ascontiguousarray(b2.reshape(8, 128).T)

    nc = build_nc()
    in_maps = []
    for c in range(NCORE):
        b_, k_ = divmod(c, 4)
        xs = np.ascontiguousarray(x[b_, 2048 * k_:2048 * (k_ + 1), :])
        pm = np.zeros((128, 4), np.float32)
        pm[:, k_] = 1.0
        gm = np.zeros((128, 2), np.float32)
        gm[:, 0 if b_ == 0 else 1] = 1.0
        in_maps.append(dict(
            x_in=xs, wz_in=Wz, wh_in=Wh, bz_in=bz2, bh_in=bh2,
            w0s_in=np.ascontiguousarray(W0[128 * c:128 * (c + 1), :]),
            wvs_in=np.ascontiguousarray(Wkv[1024 + 128 * c:1024 + 128 * (c + 1), :]),
            bkv_in=np.ascontiguousarray(bkv[1024 + 128 * c:1024 + 128 * (c + 1)].reshape(1, 128)),
            eta_in=eta, w1_in=w1, w2_in=w2, b1_in=b1_2, b2_in=b2_2,
            pmask_in=pm, gmask_in=gm,
        ))
    res = run_bass_kernel_spmd(nc, in_maps, core_ids=list(range(NCORE)))
    out = np.empty((2, 8192, 1024), np.float32)
    for c in range(NCORE):
        b_, k_ = divmod(c, 4)
        out[b_, 2048 * k_:2048 * (k_ + 1), :] = res.results[c]["out_d"]
    return out
